# version 12
# baseline (speedup 1.0000x reference)
import numpy as np
import ml_dtypes
from contextlib import ExitStack
import concourse.bass as bass
import concourse.mybir as mybir
from concourse.bass_utils import run_bass_kernel_spmd

F32 = mybir.dt.float32
BF16 = mybir.dt.bfloat16
I32 = mybir.dt.int32
AF = mybir.ActivationFunctionType
ALU = mybir.AluOpType
AX = mybir.AxisListType

D = 1024
KT = 8
NE = 32
NJOB = 34
CH_A = 512
CH_F = 1024
EPS = 1e-5

ENGS = ("sync", "tensor", "vector", "scalar", "gpsimd")
OWN = {"sync": None, "tensor": "sT", "vector": "sV", "scalar": "sS", "gpsimd": None}


class Prog:

    def __init__(self):
        self.q = {e: [] for e in ENGS}
        self.cnt = {}
        self.last_wait = {e: {} for e in ENGS}

    def wait(self, eng, ev, flush=False):
        if ev is None:
            return
        sem, c = ev
        if OWN[eng] == sem and not flush:
            return
        if c <= self.last_wait[eng].get(sem, -1):
            return
        self.last_wait[eng][sem] = c
        self.q[eng].append(("w", sem, c))

    def emit(self, eng, fn, inc=None, waits=()):
        for ev in waits:
            self.wait(eng, ev)
        ev = None
        if inc is not None:
            step = 1 if inc in ("sT", "sV", "sS") else 16
            self.cnt[inc] = self.cnt.get(inc, 0) + step
            ev = (inc, self.cnt[inc])
        self.q[eng].append(("i", fn, inc))
        return ev

    def run(self, eng_name, eng, sems):
        for item in self.q[eng_name]:
            if item[0] == "w":
                eng.wait_ge(sems[item[1]], item[2])
            else:
                inst = item[1](eng)
                if item[2] is not None:
                    inst.then_inc(sems[item[2]],
                                  1 if item[2] in ("sT", "sV", "sS") else 16)


def build_nc(n_seq, S, debug=False, attn_only=False):
    T = n_seq * S
    NCA = T // CH_A
    NCF = T // CH_F
    CPS = S // CH_A
    NTT = T // 128
    NT_F = CH_F // 128

    nc = bass.Bass("TRN2", target_bir_lowering=False)

    x_h = nc.dram_tensor("x", [n_seq, S, D], F32, kind="ExternalInput")
    gamma_h = nc.dram_tensor("gamma", [128, KT], F32, kind="ExternalInput")
    wqkvag_h = [nc.dram_tensor(n, [D, D], F32, kind="ExternalInput")
                for n in ("wq", "wk", "wv", "wa", "wg")]
    wo_h = nc.dram_tensor("wo", [D, D], F32, kind="ExternalInput")
    gw_h = nc.dram_tensor("gate_w", [NE, D], F32, kind="ExternalInput")
    gb_h = nc.dram_tensor("gate_b", [1, NE], F32, kind="ExternalInput")
    egu_h = nc.dram_tensor("egu", [NE, D, D], F32, kind="ExternalInput")
    edn_h = nc.dram_tensor("edn", [NE, D, 512], F32, kind="ExternalInput")
    sgw_h = nc.dram_tensor("sgw", [D, D], F32, kind="ExternalInput")
    suw_h = nc.dram_tensor("suw", [D, D], F32, kind="ExternalInput")
    sdw_h = nc.dram_tensor("sdw", [D, D], F32, kind="ExternalInput")
    y_h = nc.dram_tensor("y", [n_seq, S, D], F32, kind="ExternalOutput")

    scr_x1 = nc.dram_tensor("scr_x1", [T, D], F32)
    scr_xn2T = nc.dram_tensor("scr_xn2T", [KT, 128, T], BF16)
    scr_guT = nc.dram_tensor("scr_guT", [NJOB, KT, 128, D], BF16)
    scr_dnT = nc.dram_tensor("scr_dnT", [NJOB, 4, 128, D], BF16)

    ident_h = nc.inline_tensor(np.eye(128, dtype=ml_dtypes.bfloat16), name="ident")
    ones_h = nc.inline_tensor(np.ones((1, 128), dtype=np.float32), name="ones1")

    def gu_src(j, kk):
        if j >= 2:
            return egu_h[j - 2, kk * 128:(kk + 1) * 128, :]
        half = j
        if kk < 4:
            return sgw_h[half * 512 + kk * 128: half * 512 + (kk + 1) * 128, :]
        return suw_h[half * 512 + (kk - 4) * 128: half * 512 + (kk - 3) * 128, :]

    def dn_src(j, kk):
        if j >= 2:
            return edn_h[j - 2, kk * 128:(kk + 1) * 128, :]
        return sdw_h[kk * 128:(kk + 1) * 128, j * 512:(j + 1) * 512]

    P = Prog()
    es = ExitStack()
    sb = lambda n, s, dt=F32: es.enter_context(nc.sbuf_tensor(n, s, dt))

    gwT_b = sb("gwT_b", [128, KT, NE], BF16)
    gam_s = sb("gam_s", [128, KT])
    bias_bc = sb("bias_bc", [128, NE])
    ones_s = sb("ones_s", [1, 128])
    ident_s = sb("ident_s", [128, 128], BF16)
    zeros2 = sb("zeros2", [128, 512])
    gb_s = sb("gb_s", [1, NE])
    gw_s = sb("gw_s", [NE, D])
    gwb_s = sb("gwb_s", [NE, D], BF16)
    combine = sb("combine", [128, NTT * NE])
    carry = sb("carry", [128, KT])

    pf = [es.enter_context(nc.psum_tensor(f"pf{i}", [128, 512], F32)) for i in range(6)]
    pbt = [es.enter_context(nc.psum_tensor(f"pb{i}", [128, 512], F32)) for i in range(2)]
    pb = [pbt[i][:, :].bitcast(BF16) for i in range(2)]
    pf_free = [None] * 6
    pb_free = [None] * 2
    _nx = {"f": 0, "b": 0}

    def acq_f32():
        i = _nx["f"]; _nx["f"] = (i + 1) % 6
        return i

    def acq_bf():
        i = _nx["b"]; _nx["b"] = (i + 1) % 2
        return i

    a_es = ExitStack()
    sba = lambda n, s, dt=F32: a_es.enter_context(nc.sbuf_tensor(n, s, dt))
    xc = [sba("xc0", [128, 4, D]), sba("xc1", [128, 4, D])]
    x1c = sba("x1c", [128, 4, D])
    ssum = sba("ssum", [128, 4]); ssq = sba("ssq", [128, 4])
    ssum2 = sba("ssum2", [128, 4]); ssq2 = sba("ssq2", [128, 4])
    stA = [sba(f"stA{i}", [128, 4]) for i in range(4)]
    sA = sba("sA", [128, 4]); bA = sba("bA", [128, 4])
    sA2 = sba("sA2", [128, 4]); bA2 = sba("bA2", [128, 4])
    iN = sba("iN", [128, 4], I32)
    W_b = [sba(f"W{i}", [128, KT, D], BF16) for i in range(5)]
    wo_b = sba("wo_b", [128, KT, D], BF16)
    xn_b = sba("xn_b", [128, 4, D], BF16)
    xnT_b = sba("xnT_b", [128, KT, CH_A], BF16)
    y2T_b = sba("y2T_b", [128, KT, CH_A], BF16)
    xn2T_b = sba("xn2T_b", [128, KT, CH_A], BF16)
    s_k = [sba("s_k0", [128, CH_A])] * 2
    s_q = [sba(f"s_q{p}", [128, CH_A]) for p in range(2)]
    s_sa = [sba("s_sa0", [128, CH_A])] * 2
    s_sg = [sba("s_sg0", [128, CH_A])] * 2
    v_kv = sba("v_kv", [128, CH_A]); v_gs = sba("v_gs", [128, CH_A])
    v_y = sba("v_y", [128, CH_A])
    r_sc = [sba(f"r_sc{i}", [128, NE]) for i in range(4)]
    r_sfc = sba("r_sfc", [128, NE]); r_gmax = sba("r_gmax", [128, 8])
    r_m8 = sba("r_m8", [128, 8]); r_msk = sba("r_msk", [128, NE])
    r_tw = sba("r_tw", [128, NE]); r_dn = sba("r_dn", [128, 1])
    r_fc = sba("r_fc", [128, 1])
    a_es.close()

    st_es = ExitStack()
    stage = st_es.enter_context(nc.sbuf_tensor("stage", [128, KT, D], F32))

    ev_in = []
    ev_in.append(P.emit("sync", lambda e: e.dma_start(ident_s[:, :], ident_h[:, :]), inc="dI"))
    ev_in.append(P.emit("sync", lambda e: e.dma_start(ones_s[:, :], ones_h[:, :]), inc="dI"))
    ev_in.append(P.emit("sync", lambda e: e.dma_start(
        gam_s[:, :], gamma_h[:, :]), inc="dI"))
    ev_in.append(P.emit("sync", lambda e: e.dma_start(gb_s[:, :], gb_h[:, :]), inc="dI"))
    ev_in.append(P.emit("sync", lambda e: e.dma_start(gw_s[:, :], gw_h[:, :]), inc="dI"))
    ev_zero = P.emit("vector", lambda e: e.memset(zeros2[:, :], 0.0), inc="sV")

    ev_casts = []
    ev_stage_rd = None
    for wi, wh in enumerate(wqkvag_h + [wo_h]):
        ld = P.emit("sync", lambda e, wh=wh: e.dma_start(
            stage[:, :, :], wh.rearrange("(k p) n -> p k n", p=128)),
            inc="dW", waits=[ev_stage_rd] + ([ev_in[-1]] if wi == 0 else []))
        dst = W_b[wi] if wi < 5 else wo_b
        last = None
        for kk in range(KT):
            last = P.emit("scalar", lambda e, wi=wi, kk=kk, dst=dst: e.activation(
                dst[:, kk, :], stage[:, kk, :], AF.Identity,
                scale=(gam_s[:, kk:kk + 1] if wi < 5 else 1.0)),
                inc="sS", waits=[ld] + ([ev_in[-1]] if wi == 0 else []))
        ev_stage_rd = last
        ev_casts.append(last)
    st_es.close()

    ev_gwb = P.emit("scalar", lambda e: e.copy(gwb_s[:, :], gw_s[:, :]),
                    inc="sS", waits=[ev_in[-1]])
    bi = acq_bf()
    ev_t = None
    for kk in range(KT):
        ev_t = P.emit("tensor", lambda e, kk=kk, bi=bi: e.transpose(
            pb[bi][:, kk * NE:(kk + 1) * NE],
            gwb_s[:, kk * 128:(kk + 1) * 128], ident_s[0:32, 0:32]),
            inc="sT", waits=[ev_gwb, ev_in[-1], pb_free[bi]])
    ev_gwT = P.emit("vector", lambda e, bi=bi: e.tensor_copy(
        gwT_b[:, :, :], pb[bi][:, 0:KT * NE].rearrange("p (k n) -> p k n", k=KT)),
        inc="sV", waits=[ev_t])
    pb_free[bi] = ev_gwT
    fi = acq_f32()
    ev_bb = P.emit("tensor", lambda e, fi=fi: e.matmul(
        pf[fi][:, 0:NE], ones_s[:, :], gb_s[:, :], start=True, stop=True),
        inc="sT", waits=[ev_in[-1]])
    ev_bias = P.emit("vector", lambda e, fi=fi: e.tensor_copy(
        bias_bc[:, :], pf[fi][:, 0:NE]), inc="sV", waits=[ev_bb])
    pf_free[fi] = ev_bias

    ev_startup = ev_casts + [ev_gwT, ev_bias, ev_zero]

    def vstep(fn, waits=()):
        ev = P.emit("vector", fn, inc="sV", waits=waits)
        P.wait("vector", ev, flush=True)
        return ev

    def emit_ln(xsrc, ssum_t, ssq_t, sA_t, bA_t, attn_mode, waits_s, waits_v):
        for i in range(4):
            ev_sq = P.emit("scalar", lambda e, i=i: e.activation(
                xn_b[:, i, :], xsrc[:, i, :], AF.Square), inc="sS",
                waits=list(waits_s) if i == 0 else ())
            P.emit("vector", lambda e, i=i: e.reduce_sum(
                ssq_t[:, i:i + 1], xn_b[:, i, :], axis=AX.X),
                inc="sV", waits=[ev_sq] + (list(waits_v) if i == 0 else []))
            ev = P.emit("vector", lambda e, i=i: e.reduce_sum(
                ssum_t[:, i:i + 1], xsrc[:, i, :], axis=AX.X), inc="sV")
        P.wait("vector", ev, flush=True)
        m, msq, u, t = stA
        vstep(lambda e: e.tensor_scalar(
            m[:, :], ssum_t[:, :], 1.0 / D, None, ALU.mult))
        vstep(lambda e: e.tensor_mul(msq[:, :], m[:, :], m[:, :]))
        vstep(lambda e: e.tensor_scalar(
            u[:, :], ssq_t[:, :], 1.0 / D, None, ALU.mult))
        vstep(lambda e: e.tensor_sub(u[:, :], u[:, :], msq[:, :]))
        if attn_mode:
            vstep(lambda e: e.tensor_scalar(
                u[:, :], u[:, :], float(D) + EPS, 1e-10, ALU.mult, ALU.add))
        else:
            vstep(lambda e: e.tensor_scalar(
                u[:, :], u[:, :], 1.0, EPS, ALU.mult, ALU.add))
        vstep(lambda e: e.tensor_scalar(
            iN[:, :], u[:, :].bitcast(I32), 1, None, ALU.logical_shift_right))
        vstep(lambda e: e.tensor_scalar(
            iN[:, :], iN[:, :], 0x5F3759DF, -1, ALU.subtract, ALU.mult))
        y = iN[:, :].bitcast(F32)
        NIT = 2
        for it in range(NIT):
            vstep(lambda e: e.tensor_mul(t[:, :], y, y))
            vstep(lambda e: e.tensor_mul(t[:, :], t[:, :], u[:, :]))
            vstep(lambda e: e.tensor_scalar(
                t[:, :], t[:, :], -0.5, 1.5, ALU.mult, ALU.add))
            if it < NIT - 1:
                vstep(lambda e: e.tensor_mul(y, y, t[:, :]))
            elif attn_mode:
                vstep(lambda e: e.tensor_mul(t[:, :], y, t[:, :]))
                vstep(lambda e: e.tensor_scalar(
                    sA_t[:, :], t[:, :], float(np.sqrt(D)), None, ALU.mult))
            else:
                vstep(lambda e: e.tensor_mul(sA_t[:, :], y, t[:, :]))
        return vstep(lambda e: e.scalar_tensor_tensor(
            bA_t[:, :], m[:, :], -1.0, sA_t[:, :], ALU.mult, ALU.mult))

    def emit_transpose4(src_b, dstT, ev_src, dst_free_evs):
        evs = []
        ev_tt = None
        for i in range(4):
            bi = acq_bf()
            for ch in range(KT):
                ev_tt = P.emit("tensor", lambda e, i=i, ch=ch, bi=bi: e.transpose(
                    pb[bi][:, ch * 128:(ch + 1) * 128],
                    src_b[:, i, ch * 128:(ch + 1) * 128], ident_s[:, :]),
                    inc=("sT" if ch == KT - 1 else None),
                    waits=[ev_src, pb_free[bi]])
            ev_ev = P.emit("vector", lambda e, i=i, bi=bi: e.tensor_copy(
                dstT[:, :, i * 128:(i + 1) * 128],
                pb[bi][:, :].rearrange("p (k n) -> p k n", k=KT)),
                inc="sV", waits=[ev_tt] + list(dst_free_evs))
            pb_free[bi] = ev_ev
            evs.append(ev_ev)
        return evs, ev_tt

    ev_xc_free = [None, None]
    ev_xnT_rd = [None]
    ev_y2_rd = [None]
    ev_x1_free = [[]]
    ev_xn2T_free = [[]]
    ev_sAbA_rd = [None]
    ev_sAbA2_rd = [None]
    ev_stats_rd = [None]
    ev_stats2_rd = [None]
    ev_scr_rd = {k: [None, None] for k in ("k", "q", "sa", "sg")}
    ev_tw = [None] * 4
    ev_x1_stored = [None] * NCA
    ev_xn2T_stored = [None] * NCA
    ev_comb = [None] * NTT
    tt_done = [None] * NCA
    tt2_done = [None] * NCA
    ST = [{} for _ in range(NCA)]

    def attn_front(c):
        seq, lc = c // CPS, c % CPS
        xp = c % 2
        ev_x = P.emit("sync", lambda e, seq=seq, lc=lc, xp=xp: e.dma_start(
            xc[xp][:, :, :],
            x_h[seq, lc * CH_A:(lc + 1) * CH_A, :].rearrange("(i p) n -> p i n", p=128)),
            inc="dX%d" % xp, waits=[ev_xc_free[xp]] + (ev_startup if c <= 1 else []))
        trash_free = (tt_done[c - 1] if attn_only else
                      (tt2_done[c - 2] if c >= 2 else None)) if c >= 1 else None
        ev_snb = emit_ln(xc[xp], ssum, ssq, sA, bA, True,
                         [ev_x, trash_free],
                         [ev_stats_rd[0], ev_sAbA_rd[0]])
        ev_stats_rd[0] = ev_snb
        ev_xn = None
        for i in range(4):
            ev_xn = P.emit("scalar", lambda e, i=i, xp=xp: e.activation(
                xn_b[:, i, :], xc[xp][:, i, :], AF.Identity,
                scale=sA[:, i:i + 1], bias=bA[:, i:i + 1]),
                inc="sS", waits=[ev_snb])
        ev_sAbA_rd[0] = ev_xn
        evs, ev_tt = emit_transpose4(xn_b, xnT_b, ev_xn, [ev_xnT_rd[0]])
        tt_done[c] = ev_tt
        ST[c]["ev_xnT"] = evs[-1]

    def attn_backA(c):
        seq, lc = c // CPS, c % CPS
        ev_xnT = ST[c]["ev_xnT"]
        ev_y2 = None
        ev_last_qkvag = None
        for m in range(KT):
            mpar = m % 2
            banks = []
            for wi in range(5):
                fi2 = acq_f32()
                ev_mm = None
                for kk in range(KT):
                    ev_mm = P.emit("tensor", lambda e, wi=wi, kk=kk, m=m, fi2=fi2: e.matmul(
                        pf[fi2][:, :], W_b[wi][:, kk, m * 128:(m + 1) * 128],
                        xnT_b[:, kk, :], start=(kk == 0), stop=(kk == KT - 1)),
                        inc=("sT" if kk == KT - 1 else None),
                        waits=[ev_xnT, pf_free[fi2]])
                banks.append((fi2, ev_mm))
            ev_last_qkvag = banks[-1][1]
            (fq, evq), (fk, evk), (fv, evv), (fa, eva), (fg, evg) = banks
            ev_ek = P.emit("scalar", lambda e, fk=fk, mpar=mpar: e.copy(
                s_k[mpar][:, :], pf[fk][:, :]), inc="sS",
                waits=[evk, ev_scr_rd["k"][mpar]])
            ev_eq = P.emit("scalar", lambda e, fq=fq, mpar=mpar: e.copy(
                s_q[mpar][:, :], pf[fq][:, :]), inc="sS",
                waits=[evq, ev_scr_rd["q"][mpar]])
            ev_ea = P.emit("scalar", lambda e, fa=fa, mpar=mpar: e.activation(
                s_sa[mpar][:, :], pf[fa][:, :], AF.Sigmoid), inc="sS",
                waits=[eva, ev_scr_rd["sa"][mpar]])
            ev_eg = P.emit("scalar", lambda e, fg=fg, mpar=mpar: e.activation(
                s_sg[mpar][:, :], pf[fg][:, :], AF.Sigmoid), inc="sS",
                waits=[evg, ev_scr_rd["sg"][mpar]])
            pf_free[fk] = ev_ek
            pf_free[fq] = ev_eq
            pf_free[fa] = ev_ea
            ev_kv = P.emit("vector", lambda e, fv=fv, mpar=mpar: e.tensor_mul(
                v_kv[:, :], s_k[mpar][:, :], pf[fv][:, :]), inc="sV",
                waits=[ev_ek, evv])
            pf_free[fv] = ev_kv
            ev_gs = P.emit("vector", lambda e, fg=fg, mpar=mpar: e.tensor_mul(
                v_gs[:, :], s_sg[mpar][:, :], pf[fg][:, :]), inc="sV",
                waits=[ev_eg])
            pf_free[fg] = ev_gs
            init = 0.0 if lc == 0 else carry[:, m:m + 1]
            ev_scan = P.emit("vector", lambda e, mpar=mpar, init=init: e.tensor_tensor_scan(
                v_y[:, :], s_sa[mpar][:, :], v_kv[:, :], init, ALU.mult, ALU.add),
                inc="sV", waits=[ev_ea])
            P.wait("vector", ev_scan, flush=True)
            P.emit("vector", lambda e, m=m: e.tensor_copy(
                carry[:, m:m + 1], v_y[:, CH_A - 1:CH_A]))
            P.emit("vector", lambda e, mpar=mpar: e.tensor_mul(
                v_kv[:, :], s_q[mpar][:, :], v_y[:, :]), waits=[ev_eq])
            ev_y2 = P.emit("vector", lambda e, m=m: e.tensor_mul(
                y2T_b[:, m, :], v_kv[:, :], v_gs[:, :]), inc="sV",
                waits=[ev_y2_rd[0]])
            ev_scr_rd["k"][0] = ev_scr_rd["k"][1] = ev_kv
            ev_scr_rd["q"][mpar] = ev_y2
            ev_scr_rd["sa"][0] = ev_scr_rd["sa"][1] = ev_scan
            ev_scr_rd["sg"][0] = ev_scr_rd["sg"][1] = ev_gs
        ev_xnT_rd[0] = ev_last_qkvag
        ev_res = None
        evR = None
        for i in range(4):
            fL = acq_f32(); fR = acq_f32()
            evL = None
            for kk in range(KT):
                evL = P.emit("tensor", lambda e, i=i, kk=kk, fL=fL: e.matmul(
                    pf[fL][:, :], y2T_b[:, kk, i * 128:(i + 1) * 128],
                    wo_b[:, kk, 0:512], start=(kk == 0), stop=(kk == KT - 1)),
                    inc=("sT" if kk == KT - 1 else None),
                    waits=[ev_y2, pf_free[fL]])
                evR = P.emit("tensor", lambda e, i=i, kk=kk, fR=fR: e.matmul(
                    pf[fR][:, :], y2T_b[:, kk, i * 128:(i + 1) * 128],
                    wo_b[:, kk, 512:1024], start=(kk == 0), stop=(kk == KT - 1)),
                    inc=("sT" if kk == KT - 1 else None),
                    waits=[pf_free[fR]])
            P.emit("vector", lambda e, i=i, fL=fL, xp=c % 2: e.tensor_add(
                x1c[:, i, 0:512], xc[xp][:, i, 0:512], pf[fL][:, :]),
                waits=[evL] + ev_x1_free[0])
            ev_res = P.emit("vector", lambda e, i=i, fR=fR, xp=c % 2: e.tensor_add(
                x1c[:, i, 512:1024], xc[xp][:, i, 512:1024], pf[fR][:, :]),
                inc="sV", waits=[evR])
            pf_free[fL] = ev_res
            pf_free[fR] = ev_res
        ev_y2_rd[0] = evR
        ev_xc_free[c % 2] = ev_res
        ev_x1_stored[c] = P.emit("gpsimd", lambda e, c=c: e.dma_start(
            scr_x1[c * CH_A:(c + 1) * CH_A, :].rearrange("(i p) n -> p i n", p=128),
            x1c[:, :, :]), inc="gX", waits=[ev_res])
        ev_x1_free[0] = [ev_x1_stored[c]]
        ST[c]["ev_res"] = ev_res

    def attn_backB_pre(c):
        ev_res = ST[c]["ev_res"]
        tt_next = tt_done[c + 1] if c + 1 < NCA else tt_done[c]
        ev_snb2 = emit_ln(x1c, ssum2, ssq2, sA2, bA2, False,
                          [ev_res, tt_next],
                          [ev_stats2_rd[0], ev_sAbA2_rd[0]])
        ev_stats2_rd[0] = ev_snb2
        ev_xn2 = None
        for i in range(4):
            ev_xn2 = P.emit("scalar", lambda e, i=i: e.activation(
                xn_b[:, i, :], x1c[:, i, :], AF.Identity,
                scale=sA2[:, i:i + 1], bias=bA2[:, i:i + 1]),
                inc="sS", waits=[ev_snb2])
        ev_sAbA2_rd[0] = ev_xn2
        ev_x1_free[0].append(ev_xn2)
        ST[c]["ev_xn2"] = ev_xn2

    def attn_backB_post(c):
        ev_xn2 = ST[c]["ev_xn2"]
        evs2, ev_tt2 = emit_transpose4(xn_b, xn2T_b, ev_xn2, ev_xn2T_free[0])
        tt2_done[c] = ev_tt2
        ev_xn2T = evs2[-1]
        ev_st2 = None
        for kk in range(KT):
            ev_st2 = P.emit("gpsimd", lambda e, kk=kk, c=c: e.dma_start(
                scr_xn2T[kk, :, c * CH_A:(c + 1) * CH_A], xn2T_b[:, kk, :]),
                inc="gT", waits=[ev_xn2T])
        ev_xn2T_stored[c] = ev_st2
        ev_xn2T_free[0] = [ev_st2]
        for i in range(4):
            g = c * 4 + i
            fi3 = acq_f32()
            ev_lg = None
            for kk in range(KT):
                ev_lg = P.emit("tensor", lambda e, i=i, kk=kk, fi3=fi3: e.matmul(
                    pf[fi3][:, 0:NE], xn2T_b[:, kk, i * 128:(i + 1) * 128],
                    gwT_b[:, kk, :], start=(kk == 0), stop=(kk == KT - 1)),
                    inc=("sT" if kk == KT - 1 else None),
                    waits=[ev_xn2T, pf_free[fi3]])
            ev_xn2T_free[0].append(ev_lg)
            ev_sc = P.emit("scalar", lambda e, i=i, fi3=fi3: e.activation(
                r_sc[i][:, :], pf[fi3][:, 0:NE], AF.Sigmoid), inc="sS",
                waits=[ev_lg, ev_tw[i]])
            pf_free[fi3] = ev_sc

            def vfl(ev):
                P.wait("vector", ev, flush=True)
            ev1 = P.emit("vector", lambda e, i=i: e.tensor_add(
                r_sfc[:, :], r_sc[i][:, :], bias_bc[:, :]), inc="sV", waits=[ev_sc])
            vfl(ev1)
            ev2 = P.emit("vector", lambda e: e.reduce_max(
                r_gmax[:, :], r_sfc[:, :].rearrange("p (g r) -> p g r", r=4),
                axis=AX.X), inc="sV")
            vfl(ev2)
            ev3 = P.emit("vector", lambda e: e.max(r_m8[:, :], r_gmax[:, :]), inc="sV")
            vfl(ev3)
            ev4 = P.emit("vector", lambda e: e.scalar_tensor_tensor(
                r_msk[:, :].rearrange("p (g r) -> p g r", r=4),
                r_gmax[:, :].rearrange("p (g r) -> p g r", r=1).to_broadcast([128, 8, 4]),
                r_m8[:, 3:4],
                r_sfc[:, :].rearrange("p (g r) -> p g r", r=4),
                ALU.is_ge, ALU.mult), inc="sV")
            vfl(ev4)
            ev5 = P.emit("vector", lambda e: e.max(r_m8[:, :], r_msk[:, :]), inc="sV")
            vfl(ev5)
            ev6 = P.emit("vector", lambda e, i=i: e.scalar_tensor_tensor(
                r_tw[:, :], r_msk[:, :], r_m8[:, 5:6], r_sc[i][:, :],
                ALU.is_ge, ALU.mult), inc="sV")
            ev_tw[i] = ev6
            vfl(ev6)
            ev6b = P.emit("vector", lambda e: e.reduce_sum(
                r_dn[:, :], r_tw[:, :], axis=AX.X), inc="sV")
            vfl(ev6b)
            ev6c = P.emit("vector", lambda e: e.tensor_scalar(
                r_dn[:, :], r_dn[:, :], 1e-20, None, ALU.add), inc="sV")
            vfl(ev6c)
            ev7 = P.emit("vector", lambda e: e.reciprocal(r_fc[:, :], r_dn[:, :]), inc="sV")
            vfl(ev7)
            ev8 = P.emit("vector", lambda e: e.tensor_scalar(
                r_fc[:, :], r_fc[:, :], 2.5, None, ALU.mult), inc="sV")
            vfl(ev8)
            ev_comb[g] = P.emit("vector", lambda e, g=g: e.tensor_tensor(
                combine[:, g * NE:(g + 1) * NE], r_tw[:, :],
                r_fc[:, :].to_broadcast([128, NE]), ALU.mult), inc="sV")

    if attn_only:
        attn_front(0)
        for c in range(NCA):
            attn_backA(c)
            if c + 1 < NCA:
                attn_front(c + 1)
    else:
        attn_front(0)
        for c in range(NCA):
            attn_backA(c)
            if c + 1 < NCA:
                attn_front(c + 1)
            attn_backB_pre(c)
            attn_backB_post(c)

    if attn_only:
        dump = {
            "d_x1": (scr_x1[:, :], [T, D], F32, [ev_x1_stored[-1]]),
            "d_sA": (sA[:, :], [128, 4], F32, []),
            "d_bA": (bA[:, :], [128, 4], F32, []),
            "d_xnb": (xn_b[:, :, :], [128, 4 * D], BF16, []),
            "d_xnT": (xnT_b[:, :, :], [128, KT * CH_A], BF16, []),
            "d_y2T": (y2T_b[:, :, :], [128, KT * CH_A], BF16, []),
            "d_sa": (s_sa[1][:, :], [128, CH_A], F32, []),
            "d_sk": (s_k[1][:, :], [128, CH_A], F32, []),
            "d_sq": (s_q[1][:, :], [128, CH_A], F32, []),
            "d_sg": (s_sg[1][:, :], [128, CH_A], F32, []),
            "d_vy": (v_y[:, :], [128, CH_A], F32, []),
        }
        ev_last_res = ST[NCA - 1]["ev_res"]
        ev_d = None
        for name, (src, shp, dt, extra) in dump.items():
            oh = nc.dram_tensor(name, shp, dt, kind="ExternalOutput")
            src2 = src if len(src.shape) <= 2 else src.rearrange("p a b -> p (a b)")
            ev_d = P.emit("gpsimd", lambda e, oh=oh, src2=src2: e.dma_start(
                oh[:, :] if len(oh.shape) == 2 else oh, src2),
                inc="dB", waits=[ev_last_res] + extra)
        P.wait("gpsimd", ev_d)
        f_es0 = ExitStack()
        with es:
            sems = {s: es.enter_context(nc.semaphore(s))
                    for s in P.cnt}
            with nc.Block() as block:
                @block.sync
                def _(sync):
                    P.run("sync", sync, sems)

                @block.tensor
                def _(tensor):
                    P.run("tensor", tensor, sems)

                @block.vector
                def _(vector):
                    P.run("vector", vector, sems)

                @block.scalar
                def _(scalar):
                    P.run("scalar", scalar, sems)

                @block.gpsimd
                def _(gpsimd):
                    P.run("gpsimd", gpsimd, sems)
        return nc

    attn_done = [(s, c) for s, c in P.cnt.items() if c > 0]

    f_es = ExitStack()
    sbf = lambda n, s, dt=F32: f_es.enter_context(nc.sbuf_tensor(n, s, dt))
    xn2T_f = [sbf(f"xn2T_f{p}", [128, KT, CH_F], BF16) for p in range(2)]
    guT = [sbf(f"guT{p}", [128, KT, D], BF16) for p in range(3)]
    dnT = [sbf(f"dnT{p}", [128, 4, D], BF16) for p in range(2)]
    h_b = [sbf(f"h_b{p}", [128, 4, CH_F], BF16) for p in range(2)]
    sgl = [sbf(f"sgl{p}", [128, 512]) for p in range(2)]
    acc = sbf("acc", [128, NT_F, D])
    x1f = [sbf(f"x1f{p}", [128, D]) for p in range(2)]
    outb = [sbf(f"outb{p}", [128, D]) for p in range(2)]
    stgF = [sbf(f"stgF{p}", [128, D]) for p in range(2)]
    gub = [sbf(f"gub{p}", [128, D], BF16) for p in range(2)]

    for e in ENGS:
        for ev in attn_done:
            if ev[1] > 0:
                P.wait(e, ev)

    ev_guT_free = [[], [], []]
    ev_h_par = [None, None]
    ev_h_rd_par = [None, None]
    ev_dnT_ok_par = [None, None]
    ev_dnT_rd_par = [None, None]
    ev_dnT_stl_par = [None, None]
    ev_sgl_rd = [None, None]
    ev_xn2Tf_rd = [None, None]
    ev_x1f_rd = [None, None]
    ev_outb_st = [None, None]
    ev_stg_rd = [None, None]
    ev_gub_rd = [None, None]
    ev_guT_st = [[None] * KT for _ in range(NJOB)]
    ev_dnT_st = [[None] * 4 for _ in range(NJOB)]
    ev_out_final = None

    w_totals = [[]]
    for f in range(NCF):
        if f == 1:
            w_totals[0] = [(s, P.cnt[s]) for s in
                           ("gW0", "gW1", "gW2", "gV0", "gV1") if s in P.cnt]
        need = [ev_xn2T_stored[c]
                for c in range(f * (CH_F // CH_A), (f + 1) * (CH_F // CH_A))]
        ev_xf = None
        for kk in range(KT):
            ev_xf = P.emit("sync", lambda e, kk=kk, f=f: e.dma_start(
                xn2T_f[0][:, kk, :], scr_xn2T[kk, :, f * CH_F:(f + 1) * CH_F]),
                inc="dN0", waits=need + [ev_xn2Tf_rd[0]])
        if debug and f == 0:
            yxnf = nc.dram_tensor("y_xnf", [128, KT * CH_F], BF16,
                                  kind="ExternalOutput")
            P.emit("gpsimd", lambda e: e.dma_start(
                yxnf[:, :], xn2T_f[0][:, :, :].rearrange("p a b -> p (a b)")),
                inc="dB", waits=[ev_xf])
        ev_acc_done = None
        ev_gu_last = None

        def emit_down_part(jd, tlist, f=f):
            hp = jd % 2
            ev_dn_last = None
            ev_last_stt = None
            for t in tlist:
                g = f * NT_F + t
                fL = acq_f32(); fR = acq_f32()
                evL = evR2 = None
                for mi in range(4):
                    evL = P.emit("tensor", lambda e, t=t, mi=mi, fL=fL, hp=hp: e.matmul(
                        pf[fL][:, :], h_b[hp][:, mi, t * 128:(t + 1) * 128],
                        dnT[hp][:, mi, 0:512], start=(mi == 0), stop=(mi == 3)),
                        inc=("sT" if mi == 3 else None),
                        waits=[ev_h_par[hp], ev_dnT_ok_par[hp], pf_free[fL]])
                    evR2 = P.emit("tensor", lambda e, t=t, mi=mi, fR=fR, hp=hp: e.matmul(
                        pf[fR][:, :], h_b[hp][:, mi, t * 128:(t + 1) * 128],
                        dnT[hp][:, mi, 512:1024], start=(mi == 0), stop=(mi == 3)),
                        inc=("sT" if mi == 3 else None),
                        waits=[pf_free[fR]])
                ev_dn_last = evR2
                if jd == 0:
                    mkL = lambda e, t=t, fL=fL: e.scalar_tensor_tensor(
                        acc[:, t, 0:512], pf[fL][:, :], 1.0, zeros2[:, :],
                        ALU.mult, ALU.add)
                    mkR = lambda e, t=t, fR=fR: e.scalar_tensor_tensor(
                        acc[:, t, 512:1024], pf[fR][:, :], 1.0, zeros2[:, :],
                        ALU.mult, ALU.add)
                else:
                    cap = (1.0 if jd == 1 else
                           combine[:, g * NE + (jd - 2): g * NE + (jd - 2) + 1])
                    mkL = lambda e, t=t, fL=fL, cap=cap: e.scalar_tensor_tensor(
                        acc[:, t, 0:512], pf[fL][:, :], cap, acc[:, t, 0:512],
                        ALU.mult, ALU.add)
                    mkR = lambda e, t=t, fR=fR, cap=cap: e.scalar_tensor_tensor(
                        acc[:, t, 512:1024], pf[fR][:, :], cap, acc[:, t, 512:1024],
                        ALU.mult, ALU.add)
                evaL = P.emit("vector", mkL, inc="sV", waits=[evL, ev_comb[g]])
                evaR = P.emit("vector", mkR, inc="sV", waits=[evR2])
                pf_free[fL] = evaL
                pf_free[fR] = evaR
                ev_last_stt = evaR
            return ev_dn_last, ev_last_stt

        def emit_down_fin(jd, ev_dn_last):
            hp = jd % 2
            ev_h_rd_par[hp] = ev_dn_last
            ev_dnT_rd_par[hp] = ev_dn_last

        for j in range(NJOB):
            gp = j % 2
            hp = j % 2
            if f == 0:
                ev_guT_w = None
                for kk in range(KT):
                    sp = kk % 2
                    ev_ld = P.emit("sync", lambda e, j=j, kk=kk, sp=sp: e.dma_start(
                        stgF[sp][:, :], gu_src(j, kk)), inc="dS%d" % sp,
                        waits=[ev_stg_rd[sp]])
                    ev_cast = P.emit("scalar", lambda e, kk=kk, sp=sp: e.copy(
                        gub[sp][:, :], stgF[sp][:, :]), inc="sS",
                        waits=[ev_ld, ev_gub_rd[sp]])
                    ev_stg_rd[sp] = ev_cast
                    bi2 = acq_bf()
                    ev_tt = None
                    for ib in range(KT):
                        ev_tt = P.emit("tensor", lambda e, ib=ib, bi2=bi2, sp=sp: e.transpose(
                            pb[bi2][:, ib * 128:(ib + 1) * 128],
                            gub[sp][:, ib * 128:(ib + 1) * 128], ident_s[:, :]),
                            inc=("sT" if ib == KT - 1 else None),
                            waits=[ev_cast, pb_free[bi2]])
                    ev_gub_rd[sp] = ev_tt
                    ev_guT_w = P.emit("vector", lambda e, kk=kk, bi2=bi2, gp=gp: e.tensor_copy(
                        guT[gp][:, :, kk * 128:(kk + 1) * 128],
                        pb[bi2][:, :].rearrange("p (k n) -> p k n", k=KT)),
                        inc="sV", waits=[ev_tt] + ev_guT_free[gp])
                    pb_free[bi2] = ev_guT_w
                st_evs = []
                for kk in range(KT):
                    ev_s = P.emit("gpsimd", lambda e, j=j, kk=kk, gp=gp: e.dma_start(
                        scr_guT[j, kk, :, :], guT[gp][:, kk, :]), inc="gW%d" % gp,
                        waits=[ev_guT_w])
                    ev_guT_st[j][kk] = ev_s
                    st_evs.append(ev_s)
                ev_guT_ok = ev_guT_w
                guT_store_ev = st_evs[-1]
                ev_dnT_w = None
                for kk in range(KT):
                    sp = kk % 2
                    ev_ld = P.emit("sync", lambda e, j=j, kk=kk, sp=sp: e.dma_start(
                        stgF[sp][:, 0:512], dn_src(j, kk)), inc="dS%d" % sp,
                        waits=[ev_stg_rd[sp]])
                    ev_cast = P.emit("scalar", lambda e, kk=kk, sp=sp: e.copy(
                        gub[sp][:, 0:512], stgF[sp][:, 0:512]), inc="sS",
                        waits=[ev_ld, ev_gub_rd[sp]])
                    ev_stg_rd[sp] = ev_cast
                    bi2 = acq_bf()
                    ev_tt = None
                    for ib in range(4):
                        ev_tt = P.emit("tensor", lambda e, ib=ib, bi2=bi2, sp=sp: e.transpose(
                            pb[bi2][:, ib * 128:(ib + 1) * 128],
                            gub[sp][:, ib * 128:(ib + 1) * 128], ident_s[:, :]),
                            inc=("sT" if ib == 3 else None),
                            waits=[ev_cast, pb_free[bi2]])
                    ev_gub_rd[sp] = ev_tt
                    ev_dnT_w = P.emit("vector", lambda e, kk=kk, bi2=bi2, hp=hp: e.tensor_copy(
                        dnT[hp][:, :, kk * 128:(kk + 1) * 128],
                        pb[bi2][:, 0:512].rearrange("p (k n) -> p k n", k=4)),
                        inc="sV",
                        waits=[ev_tt, ev_dnT_rd_par[hp], ev_dnT_stl_par[hp]])
                    pb_free[bi2] = ev_dnT_w
                st_evs = []
                for kk in range(4):
                    ev_s = P.emit("gpsimd", lambda e, j=j, kk=kk, hp=hp: e.dma_start(
                        scr_dnT[j, kk, :, :], dnT[hp][:, kk, :]), inc="gV%d" % hp,
                        waits=[ev_dnT_w])
                    ev_dnT_st[j][kk] = ev_s
                    st_evs.append(ev_s)
                ev_dnT_ok_par[hp] = ev_dnT_w
                ev_dnT_stl_par[hp] = st_evs[-1]
            else:
                guT_store_ev = None
                ev_guT_ok = None
                for kk in range(KT):
                    ev_guT_ok = P.emit("sync", lambda e, j=j, kk=kk, gp=gp: e.dma_start(
                        guT[gp][:, kk, :], scr_guT[j, kk, :, :]), inc="dG%d" % gp,
                        waits=ev_guT_free[gp] + w_totals[0])
                ev_guT_free[gp] = []
                ev_dnT_ok = None
                for kk in range(4):
                    ev_dnT_ok = P.emit("sync", lambda e, j=j, kk=kk, hp=hp: e.dma_start(
                        dnT[hp][:, kk, :], scr_dnT[j, kk, :, :]), inc="dD%d" % hp,
                        waits=[ev_dnT_rd_par[hp]] + w_totals[0])
                ev_dnT_ok_par[hp] = ev_dnT_ok
                ev_dnT_stl_par[hp] = None
            ev_h = None
            ev_dnl = None
            for mi in range(4):
                fb = [acq_f32() for _ in range(4)]
                evb = [None] * 4
                for kk in range(KT):
                    for gu_i in range(2):
                        col0 = 512 * gu_i + mi * 128
                        for th in range(2):
                            bidx = gu_i * 2 + th
                            evb[bidx] = P.emit("tensor",
                                lambda e, kk=kk, col0=col0, th=th, b=fb[bidx], gp=gp: e.matmul(
                                    pf[b][:, :], guT[gp][:, kk, col0:col0 + 128],
                                    xn2T_f[0][:, kk, th * 512:(th + 1) * 512],
                                    start=(kk == 0), stop=(kk == KT - 1)),
                                inc=("sT" if kk == KT - 1 else None),
                                waits=[ev_guT_ok, ev_xf, pf_free[fb[bidx]]])
                ev_gu_last = evb[3]
                for th in range(2):
                    fg_, fu_ = fb[th], fb[2 + th]
                    spp = th
                    ev_sl = P.emit("scalar", lambda e, fg_=fg_, spp=spp: e.activation(
                        sgl[spp][:, :], pf[fg_][:, :], AF.Silu), inc="sS",
                        waits=[evb[th], ev_sgl_rd[spp]])
                    pf_free[fg_] = ev_sl
                    ev_h = P.emit("vector", lambda e, mi=mi, th=th, fu_=fu_, spp=spp, hp=hp: e.tensor_mul(
                        h_b[hp][:, mi, th * 512:(th + 1) * 512], sgl[spp][:, :],
                        pf[fu_][:, :]), inc="sV",
                        waits=[ev_sl, evb[2 + th], ev_h_rd_par[hp]])
                    pf_free[fu_] = ev_h
                    ev_sgl_rd[spp] = ev_h
            ev_h_par[hp] = ev_h
            ev_guT_free[gp] = [ev_gu_last] + ([guT_store_ev] if guT_store_ev else [])
            if j > 0:
                ev_dnl, _ = emit_down_part(j - 1, list(range(NT_F)))
                emit_down_fin(j - 1, ev_dnl)
            if debug and f == 0 and j == 2:
                ygu = nc.dram_tensor("y_gu2", [128, KT * D], BF16,
                                     kind="ExternalOutput")
                ydn = nc.dram_tensor("y_dn2", [128, 4 * D], BF16,
                                     kind="ExternalOutput")
                yh = nc.dram_tensor("y_h2", [128, 4 * CH_F], BF16,
                                    kind="ExternalOutput")
                ev_d1 = P.emit("gpsimd", lambda e: e.dma_start(
                    ygu[:, :], guT[gp][:, :, :].rearrange("p a b -> p (a b)")),
                    inc="dB", waits=[ev_guT_ok])
                ev_d2 = P.emit("gpsimd", lambda e: e.dma_start(
                    ydn[:, :], dnT[hp][:, :, :].rearrange("p a b -> p (a b)")),
                    inc="dB", waits=[ev_dnT_ok_par[hp]])
                ev_d3 = P.emit("gpsimd", lambda e: e.dma_start(
                    yh[:, :], h_b[hp][:, :, :].rearrange("p a b -> p (a b)")),
                    inc="dB", waits=[ev_h])
                ev_guT_free[gp].append(ev_d1)
                ev_dnT_rd_par[hp] = ev_d2
                ev_h_rd_par[hp] = ev_d3
        ev_dnl, ev_acc_done = emit_down_part(NJOB - 1, list(range(NT_F)))
        emit_down_fin(NJOB - 1, ev_dnl)
        ev_xn2Tf_rd[0] = ev_gu_last
        for t in range(NT_F):
            tp = t % 2
            row0 = f * CH_F + t * 128
            ev_lx = P.emit("sync", lambda e, row0=row0, tp=tp: e.dma_start(
                x1f[tp][:, :], scr_x1[row0:row0 + 128, :]), inc="dF%d" % tp,
                waits=[ev_x1_stored[row0 // CH_A], ev_x1f_rd[tp]])
            ev_o = P.emit("vector", lambda e, t=t, tp=tp: e.tensor_add(
                outb[tp][:, :], acc[:, t, :], x1f[tp][:, :]), inc="sV",
                waits=[ev_lx, ev_acc_done, ev_outb_st[tp]])
            ev_x1f_rd[tp] = ev_o
            seq, r = row0 // S, row0 % S
            ev_out_final = P.emit("gpsimd", lambda e, seq=seq, r=r, tp=tp: e.dma_start(
                y_h[seq, r:r + 128, :], outb[tp][:, :]), inc="gO%d" % tp, waits=[ev_o])
            ev_outb_st[tp] = ev_out_final

    if debug:
        yx1 = nc.dram_tensor("y_x1", [T, D], F32, kind="ExternalOutput")
        ycb = nc.dram_tensor("y_comb", [128, NTT * NE], F32, kind="ExternalOutput")
        yxn = nc.dram_tensor("y_xn2T", [KT, 128, T], BF16, kind="ExternalOutput")
        ev_d1 = P.emit("gpsimd", lambda e: e.dma_start(yx1[:, :], scr_x1[:, :]),
                       inc="dB", waits=[ev_out_final] + list(ev_x1_stored))
        ev_d2 = P.emit("gpsimd", lambda e: e.dma_start(ycb[:, :], combine[:, :]),
                       inc="dB")
        ev_d3 = P.emit("gpsimd", lambda e: e.dma_start(
            yxn.rearrange("k p t -> p k t"), scr_xn2T.rearrange("k p t -> p k t")),
            inc="dB", waits=list(ev_xn2T_stored))
        P.wait("gpsimd", ev_d3)
    P.wait("gpsimd", ev_outb_st[0])
    P.wait("gpsimd", ev_outb_st[1])
    f_es.close()

    with es:
        sems = {s: es.enter_context(nc.semaphore(s))
                for s in P.cnt}
        with nc.Block() as block:
            @block.sync
            def _(sync):
                P.run("sync", sync, sems)

            @block.tensor
            def _(tensor):
                P.run("tensor", tensor, sems)

            @block.vector
            def _(vector):
                P.run("vector", vector, sems)

            @block.scalar
            def _(scalar):
                P.run("scalar", scalar, sems)

            @block.gpsimd
            def _(gpsimd):
                P.run("gpsimd", gpsimd, sems)

    return nc


_NC_CACHE = {}


def kernel(L0_attn_x, L0_attn_gamma, L0_attn_wq, L0_attn_wk, L0_attn_wv,
           L0_attn_wa, L0_attn_wg, L0_attn_wo, L0_ffn_gate_weight,
           L0_ffn_e_score_correction_bias, L0_ffn_experts_gate_up,
           L0_ffn_experts_down, L0_ffn_shared_gate_w, L0_ffn_shared_up_w,
           L0_ffn_shared_down_w):
    f = lambda a: np.ascontiguousarray(np.asarray(a, dtype=np.float32))
    x = f(L0_attn_x)
    B, S, _ = x.shape
    n_cores = 8
    nseq = B // n_cores
    key = (nseq, S)
    if key not in _NC_CACHE:
        _NC_CACHE[key] = build_nc(nseq, S)
    nc = _NC_CACHE[key]
    base = {
        "gamma": np.ascontiguousarray(f(L0_attn_gamma).reshape(KT, 128).T),
        "wq": f(L0_attn_wq), "wk": f(L0_attn_wk), "wv": f(L0_attn_wv),
        "wa": f(L0_attn_wa), "wg": f(L0_attn_wg), "wo": f(L0_attn_wo),
        "gate_w": f(L0_ffn_gate_weight),
        "gate_b": f(L0_ffn_e_score_correction_bias).reshape(1, NE),
        "egu": f(L0_ffn_experts_gate_up), "edn": f(L0_ffn_experts_down),
        "sgw": f(L0_ffn_shared_gate_w), "suw": f(L0_ffn_shared_up_w),
        "sdw": f(L0_ffn_shared_down_w),
    }
    in_maps = []
    for ci in range(n_cores):
        m = dict(base)
        m["x"] = np.ascontiguousarray(x[ci * nseq:(ci + 1) * nseq])
        in_maps.append(m)
    res = run_bass_kernel_spmd(nc, in_maps, core_ids=list(range(n_cores)))
    out = np.concatenate([np.asarray(r["y"]) for r in res.results], axis=0)
    return out.astype(np.float32)


# version 13
# speedup vs baseline: 1.0168x; 1.0168x over previous
import numpy as np
import ml_dtypes
from contextlib import ExitStack
import concourse.bass as bass
import concourse.mybir as mybir
from concourse.bass_utils import run_bass_kernel_spmd

F32 = mybir.dt.float32
BF16 = mybir.dt.bfloat16
I32 = mybir.dt.int32
AF = mybir.ActivationFunctionType
ALU = mybir.AluOpType
AX = mybir.AxisListType

D = 1024
KT = 8
NE = 32
NJOB = 34
CH_A = 512
CH_F = 1024
EPS = 1e-5

ENGS = ("sync", "tensor", "vector", "scalar", "gpsimd")
OWN = {"sync": None, "tensor": "sT", "vector": "sV", "scalar": "sS", "gpsimd": None}


class Prog:

    def __init__(self):
        self.q = {e: [] for e in ENGS}
        self.cnt = {}
        self.last_wait = {e: {} for e in ENGS}

    def wait(self, eng, ev, flush=False):
        if ev is None:
            return
        sem, c = ev
        if OWN[eng] == sem and not flush:
            return
        if c <= self.last_wait[eng].get(sem, -1):
            return
        self.last_wait[eng][sem] = c
        self.q[eng].append(("w", sem, c))

    def emit(self, eng, fn, inc=None, waits=()):
        for ev in waits:
            self.wait(eng, ev)
        ev = None
        if inc is not None:
            step = 1 if inc in ("sT", "sV", "sS") else 16
            self.cnt[inc] = self.cnt.get(inc, 0) + step
            ev = (inc, self.cnt[inc])
        self.q[eng].append(("i", fn, inc))
        return ev

    def run(self, eng_name, eng, sems):
        for item in self.q[eng_name]:
            if item[0] == "w":
                eng.wait_ge(sems[item[1]], item[2])
            else:
                inst = item[1](eng)
                if item[2] is not None:
                    inst.then_inc(sems[item[2]],
                                  1 if item[2] in ("sT", "sV", "sS") else 16)


def build_nc(n_seq, S, debug=False, attn_only=False):
    T = n_seq * S
    NCA = T // CH_A
    NCF = T // CH_F
    CPS = S // CH_A
    NTT = T // 128
    NT_F = CH_F // 128

    nc = bass.Bass("TRN2", target_bir_lowering=False)

    x_h = nc.dram_tensor("x", [n_seq, S, D], F32, kind="ExternalInput")
    gamma_h = nc.dram_tensor("gamma", [128, KT], F32, kind="ExternalInput")
    wqkvag_h = [nc.dram_tensor(n, [D, D], F32, kind="ExternalInput")
                for n in ("wq", "wk", "wv", "wa", "wg")]
    wo_h = nc.dram_tensor("wo", [D, D], F32, kind="ExternalInput")
    gw_h = nc.dram_tensor("gate_w", [NE, D], F32, kind="ExternalInput")
    gb_h = nc.dram_tensor("gate_b", [1, NE], F32, kind="ExternalInput")
    egu_h = nc.dram_tensor("egu", [NE, D, D], F32, kind="ExternalInput")
    edn_h = nc.dram_tensor("edn", [NE, D, 512], F32, kind="ExternalInput")
    sgw_h = nc.dram_tensor("sgw", [D, D], F32, kind="ExternalInput")
    suw_h = nc.dram_tensor("suw", [D, D], F32, kind="ExternalInput")
    sdw_h = nc.dram_tensor("sdw", [D, D], F32, kind="ExternalInput")
    y_h = nc.dram_tensor("y", [n_seq, S, D], F32, kind="ExternalOutput")

    scr_x1 = nc.dram_tensor("scr_x1", [T, D], F32)
    scr_xn2T = nc.dram_tensor("scr_xn2T", [KT, 128, T], BF16)
    scr_guT = nc.dram_tensor("scr_guT", [NJOB, KT, 128, D], BF16)
    scr_dnT = nc.dram_tensor("scr_dnT", [NJOB, 4, 128, D], BF16)

    ident_h = nc.inline_tensor(np.eye(128, dtype=ml_dtypes.bfloat16), name="ident")
    ones_h = nc.inline_tensor(np.ones((1, 128), dtype=np.float32), name="ones1")

    def gu_src(j, kk):
        if j >= 2:
            return egu_h[j - 2, kk * 128:(kk + 1) * 128, :]
        half = j
        if kk < 4:
            return sgw_h[half * 512 + kk * 128: half * 512 + (kk + 1) * 128, :]
        return suw_h[half * 512 + (kk - 4) * 128: half * 512 + (kk - 3) * 128, :]

    def dn_src(j, kk):
        if j >= 2:
            return edn_h[j - 2, kk * 128:(kk + 1) * 128, :]
        return sdw_h[kk * 128:(kk + 1) * 128, j * 512:(j + 1) * 512]

    P = Prog()
    es = ExitStack()
    sb = lambda n, s, dt=F32: es.enter_context(nc.sbuf_tensor(n, s, dt))

    gwT_b = sb("gwT_b", [128, KT, NE], BF16)
    gam_s = sb("gam_s", [128, KT])
    bias_bc = sb("bias_bc", [128, NE])
    ones_s = sb("ones_s", [1, 128])
    ident_s = sb("ident_s", [128, 128], BF16)
    zeros2 = sb("zeros2", [128, 512])
    gb_s = sb("gb_s", [1, NE])
    gw_s = sb("gw_s", [NE, D])
    gwb_s = sb("gwb_s", [NE, D], BF16)
    combine = sb("combine", [128, NTT * NE])
    carry = sb("carry", [128, KT])

    pf = [es.enter_context(nc.psum_tensor(f"pf{i}", [128, 512], F32)) for i in range(6)]
    pbt = [es.enter_context(nc.psum_tensor(f"pb{i}", [128, 512], F32)) for i in range(2)]
    pb = [pbt[i][:, :].bitcast(BF16) for i in range(2)]
    pf_free = [None] * 6
    pb_free = [None] * 2
    _nx = {"f": 0, "b": 0}

    def acq_f32():
        i = _nx["f"]; _nx["f"] = (i + 1) % 6
        return i

    def acq_bf():
        i = _nx["b"]; _nx["b"] = (i + 1) % 2
        return i

    a_es = ExitStack()
    sba = lambda n, s, dt=F32: a_es.enter_context(nc.sbuf_tensor(n, s, dt))
    xc = [sba("xc0", [128, 4, D]), sba("xc1", [128, 4, D])]
    x1c = sba("x1c", [128, 4, D])
    ssum = sba("ssum", [128, 4]); ssq = sba("ssq", [128, 4])
    ssum2 = sba("ssum2", [128, 4]); ssq2 = sba("ssq2", [128, 4])
    stA = [sba(f"stA{i}", [128, 4]) for i in range(4)]
    sA = sba("sA", [128, 4]); bA = sba("bA", [128, 4])
    sA2 = sba("sA2", [128, 4]); bA2 = sba("bA2", [128, 4])
    iN = sba("iN", [128, 4], I32)
    sA_all = sba("sA_all", [128, NTT]); bA_all = sba("bA_all", [128, NTT])
    W_b = [sba(f"W{i}", [128, KT, D], BF16) for i in range(5)]
    wo_b = sba("wo_b", [128, KT, D], BF16)
    xn_b = sba("xn_b", [128, 4, D], BF16)
    xnT_b = sba("xnT_b", [128, KT, CH_A], BF16)
    y2T_b = sba("y2T_b", [128, KT, CH_A], BF16)
    xn2T_b = sba("xn2T_b", [128, KT, CH_A], BF16)
    s_k = [sba("s_k0", [128, CH_A])] * 2
    s_q = [sba(f"s_q{p}", [128, CH_A]) for p in range(2)]
    s_sa = [sba("s_sa0", [128, CH_A])] * 2
    s_sg = [sba("s_sg0", [128, CH_A])] * 2
    v_kv = sba("v_kv", [128, CH_A]); v_gs = sba("v_gs", [128, CH_A])
    v_y = sba("v_y", [128, CH_A])
    r_sc = [sba(f"r_sc{i}", [128, NE]) for i in range(4)]
    r_sfc = sba("r_sfc", [128, NE]); r_gmax = sba("r_gmax", [128, 8])
    r_m8 = sba("r_m8", [128, 8]); r_msk = sba("r_msk", [128, NE])
    r_tw = sba("r_tw", [128, NE]); r_dn = sba("r_dn", [128, 1])
    r_fc = sba("r_fc", [128, 1])
    a_es.close()

    st_es = ExitStack()
    stage = st_es.enter_context(nc.sbuf_tensor("stage", [128, KT, D], F32))

    ev_in = []
    ev_in.append(P.emit("sync", lambda e: e.dma_start(ident_s[:, :], ident_h[:, :]), inc="dI"))
    ev_in.append(P.emit("sync", lambda e: e.dma_start(ones_s[:, :], ones_h[:, :]), inc="dI"))
    ev_in.append(P.emit("sync", lambda e: e.dma_start(
        gam_s[:, :], gamma_h[:, :]), inc="dI"))
    ev_in.append(P.emit("sync", lambda e: e.dma_start(gb_s[:, :], gb_h[:, :]), inc="dI"))
    ev_in.append(P.emit("sync", lambda e: e.dma_start(gw_s[:, :], gw_h[:, :]), inc="dI"))
    ev_zero = P.emit("vector", lambda e: e.memset(zeros2[:, :], 0.0), inc="sV")

    ev_casts = []
    ev_stage_rd = None
    for wi, wh in enumerate(wqkvag_h + [wo_h]):
        ld = P.emit("sync", lambda e, wh=wh: e.dma_start(
            stage[:, :, :], wh.rearrange("(k p) n -> p k n", p=128)),
            inc="dW", waits=[ev_stage_rd] + ([ev_in[-1]] if wi == 0 else []))
        dst = W_b[wi] if wi < 5 else wo_b
        last = None
        for kk in range(KT):
            last = P.emit("scalar", lambda e, wi=wi, kk=kk, dst=dst: e.activation(
                dst[:, kk, :], stage[:, kk, :], AF.Identity,
                scale=(gam_s[:, kk:kk + 1] if wi < 5 else 1.0)),
                inc="sS", waits=[ld] + ([ev_in[-1]] if wi == 0 else []))
        ev_stage_rd = last
        ev_casts.append(last)
    st_es.close()

    ev_gwb = P.emit("scalar", lambda e: e.copy(gwb_s[:, :], gw_s[:, :]),
                    inc="sS", waits=[ev_in[-1]])
    bi = acq_bf()
    ev_t = None
    for kk in range(KT):
        ev_t = P.emit("tensor", lambda e, kk=kk, bi=bi: e.transpose(
            pb[bi][:, kk * NE:(kk + 1) * NE],
            gwb_s[:, kk * 128:(kk + 1) * 128], ident_s[0:32, 0:32]),
            inc="sT", waits=[ev_gwb, ev_in[-1], pb_free[bi]])
    ev_gwT = P.emit("vector", lambda e, bi=bi: e.tensor_copy(
        gwT_b[:, :, :], pb[bi][:, 0:KT * NE].rearrange("p (k n) -> p k n", k=KT)),
        inc="sV", waits=[ev_t])
    pb_free[bi] = ev_gwT
    fi = acq_f32()
    ev_bb = P.emit("tensor", lambda e, fi=fi: e.matmul(
        pf[fi][:, 0:NE], ones_s[:, :], gb_s[:, :], start=True, stop=True),
        inc="sT", waits=[ev_in[-1]])
    ev_bias = P.emit("vector", lambda e, fi=fi: e.tensor_copy(
        bias_bc[:, :], pf[fi][:, 0:NE]), inc="sV", waits=[ev_bb])
    pf_free[fi] = ev_bias

    ev_startup = ev_casts + [ev_gwT, ev_bias, ev_zero]

    def vstep(fn, waits=()):
        ev = P.emit("vector", fn, inc="sV", waits=waits)
        P.wait("vector", ev, flush=True)
        return ev

    pSUM = x1c[:, 0, 0:NTT]
    pSQ = x1c[:, 0, NTT:2 * NTT]
    pM = x1c[:, 0, 2 * NTT:3 * NTT]
    pMS = x1c[:, 0, 3 * NTT:4 * NTT]
    pU = x1c[:, 0, 4 * NTT:5 * NTT]
    pT = x1c[:, 0, 5 * NTT:6 * NTT]
    pIf = x1c[:, 0, 6 * NTT:7 * NTT]
    pI = pIf.bitcast(I32)
    ev_pp_free = [None, None]
    ev_pp_sq = None
    ev_pp_red = None
    for c in range(NCA):
        seq, lc = c // CPS, c % CPS
        xp = c % 2
        ev_px = P.emit("gpsimd", lambda e, seq=seq, lc=lc, xp=xp: e.dma_start(
            xc[xp][:, :, :],
            x_h[seq, lc * CH_A:(lc + 1) * CH_A, :].rearrange("(i p) n -> p i n", p=128)),
            inc="dX%d" % xp,
            waits=[ev_pp_free[xp]] + (ev_startup if c <= 1 else []))
        for i in range(4):
            g = c * 4 + i
            ev_pp_sq = P.emit("scalar", lambda e, i=i, xp=xp: e.activation(
                xn_b[:, i, :], xc[xp][:, i, :], AF.Square), inc="sS",
                waits=[ev_px])
            P.emit("vector", lambda e, g=g, i=i: e.reduce_sum(
                x1c[:, 0, NTT + g:NTT + g + 1], xn_b[:, i, :], axis=AX.X),
                inc="sV", waits=[ev_pp_sq])
            ev_pp_red = P.emit("vector", lambda e, g=g, i=i, xp=xp: e.reduce_sum(
                x1c[:, 0, g:g + 1], xc[xp][:, i, :], axis=AX.X), inc="sV")
        ev_pp_free[xp] = ev_pp_red
    P.wait("vector", ev_pp_red, flush=True)
    vstep(lambda e: e.tensor_scalar(pM, pSUM, 1.0 / D, None, ALU.mult))
    vstep(lambda e: e.tensor_mul(pMS, pM, pM))
    vstep(lambda e: e.tensor_scalar(pU, pSQ, 1.0 / D, None, ALU.mult))
    vstep(lambda e: e.tensor_sub(pU, pU, pMS))
    vstep(lambda e: e.tensor_scalar(
        pU, pU, float(D) + EPS, 1e-10, ALU.mult, ALU.add))
    vstep(lambda e: e.tensor_scalar(
        pI, pU.bitcast(I32), 1, None, ALU.logical_shift_right))
    vstep(lambda e: e.tensor_scalar(
        pI, pI, 0x5F3759DF, -1, ALU.subtract, ALU.mult))
    for it in range(2):
        vstep(lambda e: e.tensor_mul(pT, pIf, pIf))
        vstep(lambda e: e.tensor_mul(pT, pT, pU))
        vstep(lambda e: e.tensor_scalar(pT, pT, -0.5, 1.5, ALU.mult, ALU.add))
        if it == 0:
            vstep(lambda e: e.tensor_mul(pIf, pIf, pT))
        else:
            vstep(lambda e: e.tensor_mul(pT, pIf, pT))
            vstep(lambda e: e.tensor_scalar(
                sA_all[:, :], pT, float(np.sqrt(D)), None, ALU.mult))
    ev_lnA = vstep(lambda e: e.scalar_tensor_tensor(
        bA_all[:, :], pM, -1.0, sA_all[:, :], ALU.mult, ALU.mult))

    def emit_ln(xsrc, ssum_t, ssq_t, sA_t, bA_t, attn_mode, waits_s, waits_v):
        for i in range(4):
            ev_sq = P.emit("scalar", lambda e, i=i: e.activation(
                xn_b[:, i, :], xsrc[:, i, :], AF.Square), inc="sS",
                waits=list(waits_s) if i == 0 else ())
            P.emit("vector", lambda e, i=i: e.reduce_sum(
                ssq_t[:, i:i + 1], xn_b[:, i, :], axis=AX.X),
                inc="sV", waits=[ev_sq] + (list(waits_v) if i == 0 else []))
            ev = P.emit("vector", lambda e, i=i: e.reduce_sum(
                ssum_t[:, i:i + 1], xsrc[:, i, :], axis=AX.X), inc="sV")
        P.wait("vector", ev, flush=True)
        m, msq, u, t = stA
        vstep(lambda e: e.tensor_scalar(
            m[:, :], ssum_t[:, :], 1.0 / D, None, ALU.mult))
        vstep(lambda e: e.tensor_mul(msq[:, :], m[:, :], m[:, :]))
        vstep(lambda e: e.tensor_scalar(
            u[:, :], ssq_t[:, :], 1.0 / D, None, ALU.mult))
        vstep(lambda e: e.tensor_sub(u[:, :], u[:, :], msq[:, :]))
        if attn_mode:
            vstep(lambda e: e.tensor_scalar(
                u[:, :], u[:, :], float(D) + EPS, 1e-10, ALU.mult, ALU.add))
        else:
            vstep(lambda e: e.tensor_scalar(
                u[:, :], u[:, :], 1.0, EPS, ALU.mult, ALU.add))
        vstep(lambda e: e.tensor_scalar(
            iN[:, :], u[:, :].bitcast(I32), 1, None, ALU.logical_shift_right))
        vstep(lambda e: e.tensor_scalar(
            iN[:, :], iN[:, :], 0x5F3759DF, -1, ALU.subtract, ALU.mult))
        y = iN[:, :].bitcast(F32)
        NIT = 2
        for it in range(NIT):
            vstep(lambda e: e.tensor_mul(t[:, :], y, y))
            vstep(lambda e: e.tensor_mul(t[:, :], t[:, :], u[:, :]))
            vstep(lambda e: e.tensor_scalar(
                t[:, :], t[:, :], -0.5, 1.5, ALU.mult, ALU.add))
            if it < NIT - 1:
                vstep(lambda e: e.tensor_mul(y, y, t[:, :]))
            elif attn_mode:
                vstep(lambda e: e.tensor_mul(t[:, :], y, t[:, :]))
                vstep(lambda e: e.tensor_scalar(
                    sA_t[:, :], t[:, :], float(np.sqrt(D)), None, ALU.mult))
            else:
                vstep(lambda e: e.tensor_mul(sA_t[:, :], y, t[:, :]))
        return vstep(lambda e: e.scalar_tensor_tensor(
            bA_t[:, :], m[:, :], -1.0, sA_t[:, :], ALU.mult, ALU.mult))

    def emit_transpose4(src_b, dstT, ev_src, dst_free_evs):
        evs = []
        ev_tt = None
        for i in range(4):
            bi = acq_bf()
            for ch in range(KT):
                ev_tt = P.emit("tensor", lambda e, i=i, ch=ch, bi=bi: e.transpose(
                    pb[bi][:, ch * 128:(ch + 1) * 128],
                    src_b[:, i, ch * 128:(ch + 1) * 128], ident_s[:, :]),
                    inc=("sT" if ch == KT - 1 else None),
                    waits=[ev_src, pb_free[bi]])
            ev_ev = P.emit("vector", lambda e, i=i, bi=bi: e.tensor_copy(
                dstT[:, :, i * 128:(i + 1) * 128],
                pb[bi][:, :].rearrange("p (k n) -> p k n", k=KT)),
                inc="sV", waits=[ev_tt] + list(dst_free_evs))
            pb_free[bi] = ev_ev
            evs.append(ev_ev)
        return evs, ev_tt

    ev_xc_free = [ev_pp_free[0], ev_pp_free[1]]
    ev_xnT_rd = [None]
    ev_y2_rd = [None]
    ev_x1_free = [[]]
    ev_xn2T_free = [[]]
    ev_sAbA_rd = [None]
    ev_sAbA2_rd = [None]
    ev_stats_rd = [None]
    ev_stats2_rd = [None]
    ev_scr_rd = {k: [None, None] for k in ("k", "q", "sa", "sg")}
    ev_tw = [None] * 4
    ev_x1_stored = [None] * NCA
    ev_xn2T_stored = [None] * NCA
    ev_comb = [None] * NTT
    tt_done = [None] * NCA
    tt2_done = [None] * NCA
    ST = [{} for _ in range(NCA)]

    def attn_front(c):
        seq, lc = c // CPS, c % CPS
        xp = c % 2
        ev_x = P.emit("sync", lambda e, seq=seq, lc=lc, xp=xp: e.dma_start(
            xc[xp][:, :, :],
            x_h[seq, lc * CH_A:(lc + 1) * CH_A, :].rearrange("(i p) n -> p i n", p=128)),
            inc="dX%d" % xp, waits=[ev_xc_free[xp]] + (ev_startup if c <= 1 else []))
        trash_free = (tt_done[c - 1] if attn_only else
                      (tt2_done[c - 2] if c >= 2 else None)) if c >= 1 else None
        ev_xn = None
        for i in range(4):
            g = c * 4 + i
            ev_xn = P.emit("scalar", lambda e, i=i, g=g, xp=xp: e.activation(
                xn_b[:, i, :], xc[xp][:, i, :], AF.Identity,
                scale=sA_all[:, g:g + 1], bias=bA_all[:, g:g + 1]),
                inc="sS", waits=[ev_x, trash_free, ev_lnA] if i == 0 else ())
        ev_xc_free[xp] = ev_xn
        evs, ev_tt = emit_transpose4(xn_b, xnT_b, ev_xn, [ev_xnT_rd[0]])
        tt_done[c] = ev_tt
        ST[c]["ev_xnT"] = evs[-1]

    def attn_backA(c):
        seq, lc = c // CPS, c % CPS
        ev_xnT = ST[c]["ev_xnT"]
        ev_y2 = None
        ev_last_qkvag = None
        for m in range(KT):
            mpar = m % 2
            banks = []
            for wi in range(5):
                fi2 = acq_f32()
                ev_mm = None
                for kk in range(KT):
                    ev_mm = P.emit("tensor", lambda e, wi=wi, kk=kk, m=m, fi2=fi2: e.matmul(
                        pf[fi2][:, :], W_b[wi][:, kk, m * 128:(m + 1) * 128],
                        xnT_b[:, kk, :], start=(kk == 0), stop=(kk == KT - 1)),
                        inc=("sT" if kk == KT - 1 else None),
                        waits=[ev_xnT, pf_free[fi2]])
                banks.append((fi2, ev_mm))
            ev_last_qkvag = banks[-1][1]
            (fq, evq), (fk, evk), (fv, evv), (fa, eva), (fg, evg) = banks
            ev_ek = P.emit("scalar", lambda e, fk=fk, mpar=mpar: e.copy(
                s_k[mpar][:, :], pf[fk][:, :]), inc="sS",
                waits=[evk, ev_scr_rd["k"][mpar]])
            ev_eq = P.emit("scalar", lambda e, fq=fq, mpar=mpar: e.copy(
                s_q[mpar][:, :], pf[fq][:, :]), inc="sS",
                waits=[evq, ev_scr_rd["q"][mpar]])
            ev_ea = P.emit("scalar", lambda e, fa=fa, mpar=mpar: e.activation(
                s_sa[mpar][:, :], pf[fa][:, :], AF.Sigmoid), inc="sS",
                waits=[eva, ev_scr_rd["sa"][mpar]])
            ev_eg = P.emit("scalar", lambda e, fg=fg, mpar=mpar: e.activation(
                s_sg[mpar][:, :], pf[fg][:, :], AF.Sigmoid), inc="sS",
                waits=[evg, ev_scr_rd["sg"][mpar]])
            pf_free[fk] = ev_ek
            pf_free[fq] = ev_eq
            pf_free[fa] = ev_ea
            ev_kv = P.emit("vector", lambda e, fv=fv, mpar=mpar: e.tensor_mul(
                v_kv[:, :], s_k[mpar][:, :], pf[fv][:, :]), inc="sV",
                waits=[ev_ek, evv])
            pf_free[fv] = ev_kv
            ev_gs = P.emit("vector", lambda e, fg=fg, mpar=mpar: e.tensor_mul(
                v_gs[:, :], s_sg[mpar][:, :], pf[fg][:, :]), inc="sV",
                waits=[ev_eg])
            pf_free[fg] = ev_gs
            init = 0.0 if lc == 0 else carry[:, m:m + 1]
            ev_scan = P.emit("vector", lambda e, mpar=mpar, init=init: e.tensor_tensor_scan(
                v_y[:, :], s_sa[mpar][:, :], v_kv[:, :], init, ALU.mult, ALU.add),
                inc="sV", waits=[ev_ea])
            P.wait("vector", ev_scan, flush=True)
            P.emit("vector", lambda e, m=m: e.tensor_copy(
                carry[:, m:m + 1], v_y[:, CH_A - 1:CH_A]))
            P.emit("vector", lambda e, mpar=mpar: e.tensor_mul(
                v_kv[:, :], s_q[mpar][:, :], v_y[:, :]), waits=[ev_eq])
            ev_y2 = P.emit("vector", lambda e, m=m: e.tensor_mul(
                y2T_b[:, m, :], v_kv[:, :], v_gs[:, :]), inc="sV",
                waits=[ev_y2_rd[0]])
            ev_scr_rd["k"][0] = ev_scr_rd["k"][1] = ev_kv
            ev_scr_rd["q"][mpar] = ev_y2
            ev_scr_rd["sa"][0] = ev_scr_rd["sa"][1] = ev_scan
            ev_scr_rd["sg"][0] = ev_scr_rd["sg"][1] = ev_gs
        ev_xnT_rd[0] = ev_last_qkvag
        ev_res = None
        evR = None
        for i in range(4):
            fL = acq_f32(); fR = acq_f32()
            evL = None
            for kk in range(KT):
                evL = P.emit("tensor", lambda e, i=i, kk=kk, fL=fL: e.matmul(
                    pf[fL][:, :], y2T_b[:, kk, i * 128:(i + 1) * 128],
                    wo_b[:, kk, 0:512], start=(kk == 0), stop=(kk == KT - 1)),
                    inc=("sT" if kk == KT - 1 else None),
                    waits=[ev_y2, pf_free[fL]])
                evR = P.emit("tensor", lambda e, i=i, kk=kk, fR=fR: e.matmul(
                    pf[fR][:, :], y2T_b[:, kk, i * 128:(i + 1) * 128],
                    wo_b[:, kk, 512:1024], start=(kk == 0), stop=(kk == KT - 1)),
                    inc=("sT" if kk == KT - 1 else None),
                    waits=[pf_free[fR]])
            P.emit("vector", lambda e, i=i, fL=fL, xp=c % 2: e.tensor_add(
                x1c[:, i, 0:512], xc[xp][:, i, 0:512], pf[fL][:, :]),
                waits=[evL] + ev_x1_free[0])
            ev_res = P.emit("vector", lambda e, i=i, fR=fR, xp=c % 2: e.tensor_add(
                x1c[:, i, 512:1024], xc[xp][:, i, 512:1024], pf[fR][:, :]),
                inc="sV", waits=[evR])
            pf_free[fL] = ev_res
            pf_free[fR] = ev_res
        ev_y2_rd[0] = evR
        ev_xc_free[c % 2] = ev_res
        ev_x1_stored[c] = P.emit("gpsimd", lambda e, c=c: e.dma_start(
            scr_x1[c * CH_A:(c + 1) * CH_A, :].rearrange("(i p) n -> p i n", p=128),
            x1c[:, :, :]), inc="gX", waits=[ev_res])
        ev_x1_free[0] = [ev_x1_stored[c]]
        ST[c]["ev_res"] = ev_res

    def attn_backB_pre(c):
        ev_res = ST[c]["ev_res"]
        tt_next = tt_done[c + 1] if c + 1 < NCA else tt_done[c]
        ev_snb2 = emit_ln(x1c, ssum2, ssq2, sA2, bA2, False,
                          [ev_res, tt_next],
                          [ev_stats2_rd[0], ev_sAbA2_rd[0]])
        ev_stats2_rd[0] = ev_snb2
        ev_xn2 = None
        for i in range(4):
            ev_xn2 = P.emit("scalar", lambda e, i=i: e.activation(
                xn_b[:, i, :], x1c[:, i, :], AF.Identity,
                scale=sA2[:, i:i + 1], bias=bA2[:, i:i + 1]),
                inc="sS", waits=[ev_snb2])
        ev_sAbA2_rd[0] = ev_xn2
        ev_x1_free[0].append(ev_xn2)
        ST[c]["ev_xn2"] = ev_xn2

    def attn_backB_post(c):
        ev_xn2 = ST[c]["ev_xn2"]
        evs2, ev_tt2 = emit_transpose4(xn_b, xn2T_b, ev_xn2, ev_xn2T_free[0])
        tt2_done[c] = ev_tt2
        ev_xn2T = evs2[-1]
        ev_st2 = None
        for kk in range(KT):
            ev_st2 = P.emit("gpsimd", lambda e, kk=kk, c=c: e.dma_start(
                scr_xn2T[kk, :, c * CH_A:(c + 1) * CH_A], xn2T_b[:, kk, :]),
                inc="gT", waits=[ev_xn2T])
        ev_xn2T_stored[c] = ev_st2
        ev_xn2T_free[0] = [ev_st2]
        for i in range(4):
            g = c * 4 + i
            fi3 = acq_f32()
            ev_lg = None
            for kk in range(KT):
                ev_lg = P.emit("tensor", lambda e, i=i, kk=kk, fi3=fi3: e.matmul(
                    pf[fi3][:, 0:NE], xn2T_b[:, kk, i * 128:(i + 1) * 128],
                    gwT_b[:, kk, :], start=(kk == 0), stop=(kk == KT - 1)),
                    inc=("sT" if kk == KT - 1 else None),
                    waits=[ev_xn2T, pf_free[fi3]])
            ev_xn2T_free[0].append(ev_lg)
            ev_sc = P.emit("scalar", lambda e, i=i, fi3=fi3: e.activation(
                r_sc[i][:, :], pf[fi3][:, 0:NE], AF.Sigmoid), inc="sS",
                waits=[ev_lg, ev_tw[i]])
            pf_free[fi3] = ev_sc

            def vfl(ev):
                P.wait("vector", ev, flush=True)
            ev1 = P.emit("vector", lambda e, i=i: e.tensor_add(
                r_sfc[:, :], r_sc[i][:, :], bias_bc[:, :]), inc="sV", waits=[ev_sc])
            vfl(ev1)
            ev2 = P.emit("vector", lambda e: e.reduce_max(
                r_gmax[:, :], r_sfc[:, :].rearrange("p (g r) -> p g r", r=4),
                axis=AX.X), inc="sV")
            vfl(ev2)
            ev3 = P.emit("vector", lambda e: e.max(r_m8[:, :], r_gmax[:, :]), inc="sV")
            vfl(ev3)
            ev4 = P.emit("vector", lambda e: e.scalar_tensor_tensor(
                r_msk[:, :].rearrange("p (g r) -> p g r", r=4),
                r_gmax[:, :].rearrange("p (g r) -> p g r", r=1).to_broadcast([128, 8, 4]),
                r_m8[:, 3:4],
                r_sfc[:, :].rearrange("p (g r) -> p g r", r=4),
                ALU.is_ge, ALU.mult), inc="sV")
            vfl(ev4)
            ev5 = P.emit("vector", lambda e: e.max(r_m8[:, :], r_msk[:, :]), inc="sV")
            vfl(ev5)
            ev6 = P.emit("vector", lambda e, i=i: e.scalar_tensor_tensor(
                r_tw[:, :], r_msk[:, :], r_m8[:, 5:6], r_sc[i][:, :],
                ALU.is_ge, ALU.mult), inc="sV")
            ev_tw[i] = ev6
            vfl(ev6)
            ev6b = P.emit("vector", lambda e: e.reduce_sum(
                r_dn[:, :], r_tw[:, :], axis=AX.X), inc="sV")
            vfl(ev6b)
            ev6c = P.emit("vector", lambda e: e.tensor_scalar(
                r_dn[:, :], r_dn[:, :], 1e-20, None, ALU.add), inc="sV")
            vfl(ev6c)
            ev7 = P.emit("vector", lambda e: e.reciprocal(r_fc[:, :], r_dn[:, :]), inc="sV")
            vfl(ev7)
            ev8 = P.emit("vector", lambda e: e.tensor_scalar(
                r_fc[:, :], r_fc[:, :], 2.5, None, ALU.mult), inc="sV")
            vfl(ev8)
            ev_comb[g] = P.emit("vector", lambda e, g=g: e.tensor_tensor(
                combine[:, g * NE:(g + 1) * NE], r_tw[:, :],
                r_fc[:, :].to_broadcast([128, NE]), ALU.mult), inc="sV")

    if attn_only:
        attn_front(0)
        for c in range(NCA):
            attn_backA(c)
            if c + 1 < NCA:
                attn_front(c + 1)
    else:
        attn_front(0)
        for c in range(NCA):
            attn_backA(c)
            if c + 1 < NCA:
                attn_front(c + 1)
            attn_backB_pre(c)
            attn_backB_post(c)

    if attn_only:
        dump = {
            "d_x1": (scr_x1[:, :], [T, D], F32, [ev_x1_stored[-1]]),
            "d_sA": (sA[:, :], [128, 4], F32, []),
            "d_bA": (bA[:, :], [128, 4], F32, []),
            "d_xnb": (xn_b[:, :, :], [128, 4 * D], BF16, []),
            "d_xnT": (xnT_b[:, :, :], [128, KT * CH_A], BF16, []),
            "d_y2T": (y2T_b[:, :, :], [128, KT * CH_A], BF16, []),
            "d_sa": (s_sa[1][:, :], [128, CH_A], F32, []),
            "d_sk": (s_k[1][:, :], [128, CH_A], F32, []),
            "d_sq": (s_q[1][:, :], [128, CH_A], F32, []),
            "d_sg": (s_sg[1][:, :], [128, CH_A], F32, []),
            "d_vy": (v_y[:, :], [128, CH_A], F32, []),
        }
        ev_last_res = ST[NCA - 1]["ev_res"]
        ev_d = None
        for name, (src, shp, dt, extra) in dump.items():
            oh = nc.dram_tensor(name, shp, dt, kind="ExternalOutput")
            src2 = src if len(src.shape) <= 2 else src.rearrange("p a b -> p (a b)")
            ev_d = P.emit("gpsimd", lambda e, oh=oh, src2=src2: e.dma_start(
                oh[:, :] if len(oh.shape) == 2 else oh, src2),
                inc="dB", waits=[ev_last_res] + extra)
        P.wait("gpsimd", ev_d)
        f_es0 = ExitStack()
        with es:
            sems = {s: es.enter_context(nc.semaphore(s))
                    for s in P.cnt}
            with nc.Block() as block:
                @block.sync
                def _(sync):
                    P.run("sync", sync, sems)

                @block.tensor
                def _(tensor):
                    P.run("tensor", tensor, sems)

                @block.vector
                def _(vector):
                    P.run("vector", vector, sems)

                @block.scalar
                def _(scalar):
                    P.run("scalar", scalar, sems)

                @block.gpsimd
                def _(gpsimd):
                    P.run("gpsimd", gpsimd, sems)
        return nc

    attn_done = [(s, c) for s, c in P.cnt.items() if c > 0]

    f_es = ExitStack()
    sbf = lambda n, s, dt=F32: f_es.enter_context(nc.sbuf_tensor(n, s, dt))
    xn2T_f = [sbf(f"xn2T_f{p}", [128, KT, CH_F], BF16) for p in range(2)]
    guT = [sbf(f"guT{p}", [128, KT, D], BF16) for p in range(3)]
    dnT = [sbf(f"dnT{p}", [128, 4, D], BF16) for p in range(2)]
    h_b = [sbf(f"h_b{p}", [128, 4, CH_F], BF16) for p in range(2)]
    sgl = [sbf(f"sgl{p}", [128, 512]) for p in range(2)]
    acc = sbf("acc", [128, NT_F, D])
    x1f = [sbf(f"x1f{p}", [128, D]) for p in range(2)]
    outb = [sbf(f"outb{p}", [128, D]) for p in range(2)]
    stgF = [sbf(f"stgF{p}", [128, D]) for p in range(2)]
    gub = [sbf(f"gub{p}", [128, D], BF16) for p in range(2)]

    for e in ENGS:
        for ev in attn_done:
            if ev[1] > 0:
                P.wait(e, ev)

    ev_guT_free = [[], [], []]
    ev_h_par = [None, None]
    ev_h_rd_par = [None, None]
    ev_dnT_ok_par = [None, None]
    ev_dnT_rd_par = [None, None]
    ev_dnT_stl_par = [None, None]
    ev_sgl_rd = [None, None]
    ev_xn2Tf_rd = [None, None]
    ev_x1f_rd = [None, None]
    ev_outb_st = [None, None]
    ev_stg_rd = [None, None]
    ev_gub_rd = [None, None]
    ev_guT_st = [[None] * KT for _ in range(NJOB)]
    ev_dnT_st = [[None] * 4 for _ in range(NJOB)]
    ev_out_final = None

    w_totals = [[]]
    for f in range(NCF):
        if f == 1:
            w_totals[0] = [(s, P.cnt[s]) for s in
                           ("gW0", "gW1", "gW2", "gV0", "gV1") if s in P.cnt]
        need = [ev_xn2T_stored[c]
                for c in range(f * (CH_F // CH_A), (f + 1) * (CH_F // CH_A))]
        ev_xf = None
        for kk in range(KT):
            ev_xf = P.emit("sync", lambda e, kk=kk, f=f: e.dma_start(
                xn2T_f[0][:, kk, :], scr_xn2T[kk, :, f * CH_F:(f + 1) * CH_F]),
                inc="dN0", waits=need + [ev_xn2Tf_rd[0]])
        if debug and f == 0:
            yxnf = nc.dram_tensor("y_xnf", [128, KT * CH_F], BF16,
                                  kind="ExternalOutput")
            P.emit("gpsimd", lambda e: e.dma_start(
                yxnf[:, :], xn2T_f[0][:, :, :].rearrange("p a b -> p (a b)")),
                inc="dB", waits=[ev_xf])
        ev_acc_done = None
        ev_gu_last = None

        def emit_down_part(jd, tlist, f=f):
            hp = jd % 2
            ev_dn_last = None
            ev_last_stt = None
            for t in tlist:
                g = f * NT_F + t
                fL = acq_f32(); fR = acq_f32()
                evL = evR2 = None
                for mi in range(4):
                    evL = P.emit("tensor", lambda e, t=t, mi=mi, fL=fL, hp=hp: e.matmul(
                        pf[fL][:, :], h_b[hp][:, mi, t * 128:(t + 1) * 128],
                        dnT[hp][:, mi, 0:512], start=(mi == 0), stop=(mi == 3)),
                        inc=("sT" if mi == 3 else None),
                        waits=[ev_h_par[hp], ev_dnT_ok_par[hp], pf_free[fL]])
                    evR2 = P.emit("tensor", lambda e, t=t, mi=mi, fR=fR, hp=hp: e.matmul(
                        pf[fR][:, :], h_b[hp][:, mi, t * 128:(t + 1) * 128],
                        dnT[hp][:, mi, 512:1024], start=(mi == 0), stop=(mi == 3)),
                        inc=("sT" if mi == 3 else None),
                        waits=[pf_free[fR]])
                ev_dn_last = evR2
                if jd == 0:
                    mkL = lambda e, t=t, fL=fL: e.scalar_tensor_tensor(
                        acc[:, t, 0:512], pf[fL][:, :], 1.0, zeros2[:, :],
                        ALU.mult, ALU.add)
                    mkR = lambda e, t=t, fR=fR: e.scalar_tensor_tensor(
                        acc[:, t, 512:1024], pf[fR][:, :], 1.0, zeros2[:, :],
                        ALU.mult, ALU.add)
                else:
                    cap = (1.0 if jd == 1 else
                           combine[:, g * NE + (jd - 2): g * NE + (jd - 2) + 1])
                    mkL = lambda e, t=t, fL=fL, cap=cap: e.scalar_tensor_tensor(
                        acc[:, t, 0:512], pf[fL][:, :], cap, acc[:, t, 0:512],
                        ALU.mult, ALU.add)
                    mkR = lambda e, t=t, fR=fR, cap=cap: e.scalar_tensor_tensor(
                        acc[:, t, 512:1024], pf[fR][:, :], cap, acc[:, t, 512:1024],
                        ALU.mult, ALU.add)
                evaL = P.emit("vector", mkL, inc="sV", waits=[evL, ev_comb[g]])
                evaR = P.emit("vector", mkR, inc="sV", waits=[evR2])
                pf_free[fL] = evaL
                pf_free[fR] = evaR
                ev_last_stt = evaR
            return ev_dn_last, ev_last_stt

        def emit_down_fin(jd, ev_dn_last):
            hp = jd % 2
            ev_h_rd_par[hp] = ev_dn_last
            ev_dnT_rd_par[hp] = ev_dn_last

        for j in range(NJOB):
            gp = j % 2
            hp = j % 2
            if f == 0:
                ev_guT_w = None
                for kk in range(KT):
                    sp = kk % 2
                    ev_ld = P.emit("sync", lambda e, j=j, kk=kk, sp=sp: e.dma_start(
                        stgF[sp][:, :], gu_src(j, kk)), inc="dS%d" % sp,
                        waits=[ev_stg_rd[sp]])
                    ev_cast = P.emit("scalar", lambda e, kk=kk, sp=sp: e.copy(
                        gub[sp][:, :], stgF[sp][:, :]), inc="sS",
                        waits=[ev_ld, ev_gub_rd[sp]])
                    ev_stg_rd[sp] = ev_cast
                    bi2 = acq_bf()
                    ev_tt = None
                    for ib in range(KT):
                        ev_tt = P.emit("tensor", lambda e, ib=ib, bi2=bi2, sp=sp: e.transpose(
                            pb[bi2][:, ib * 128:(ib + 1) * 128],
                            gub[sp][:, ib * 128:(ib + 1) * 128], ident_s[:, :]),
                            inc=("sT" if ib == KT - 1 else None),
                            waits=[ev_cast, pb_free[bi2]])
                    ev_gub_rd[sp] = ev_tt
                    ev_guT_w = P.emit("vector", lambda e, kk=kk, bi2=bi2, gp=gp: e.tensor_copy(
                        guT[gp][:, :, kk * 128:(kk + 1) * 128],
                        pb[bi2][:, :].rearrange("p (k n) -> p k n", k=KT)),
                        inc="sV", waits=[ev_tt] + ev_guT_free[gp])
                    pb_free[bi2] = ev_guT_w
                st_evs = []
                for kk in range(KT):
                    ev_s = P.emit("gpsimd", lambda e, j=j, kk=kk, gp=gp: e.dma_start(
                        scr_guT[j, kk, :, :], guT[gp][:, kk, :]), inc="gW%d" % gp,
                        waits=[ev_guT_w])
                    ev_guT_st[j][kk] = ev_s
                    st_evs.append(ev_s)
                ev_guT_ok = ev_guT_w
                guT_store_ev = st_evs[-1]
                ev_dnT_w = None
                for kk in range(KT):
                    sp = kk % 2
                    ev_ld = P.emit("sync", lambda e, j=j, kk=kk, sp=sp: e.dma_start(
                        stgF[sp][:, 0:512], dn_src(j, kk)), inc="dS%d" % sp,
                        waits=[ev_stg_rd[sp]])
                    ev_cast = P.emit("scalar", lambda e, kk=kk, sp=sp: e.copy(
                        gub[sp][:, 0:512], stgF[sp][:, 0:512]), inc="sS",
                        waits=[ev_ld, ev_gub_rd[sp]])
                    ev_stg_rd[sp] = ev_cast
                    bi2 = acq_bf()
                    ev_tt = None
                    for ib in range(4):
                        ev_tt = P.emit("tensor", lambda e, ib=ib, bi2=bi2, sp=sp: e.transpose(
                            pb[bi2][:, ib * 128:(ib + 1) * 128],
                            gub[sp][:, ib * 128:(ib + 1) * 128], ident_s[:, :]),
                            inc=("sT" if ib == 3 else None),
                            waits=[ev_cast, pb_free[bi2]])
                    ev_gub_rd[sp] = ev_tt
                    ev_dnT_w = P.emit("vector", lambda e, kk=kk, bi2=bi2, hp=hp: e.tensor_copy(
                        dnT[hp][:, :, kk * 128:(kk + 1) * 128],
                        pb[bi2][:, 0:512].rearrange("p (k n) -> p k n", k=4)),
                        inc="sV",
                        waits=[ev_tt, ev_dnT_rd_par[hp], ev_dnT_stl_par[hp]])
                    pb_free[bi2] = ev_dnT_w
                st_evs = []
                for kk in range(4):
                    ev_s = P.emit("gpsimd", lambda e, j=j, kk=kk, hp=hp: e.dma_start(
                        scr_dnT[j, kk, :, :], dnT[hp][:, kk, :]), inc="gV%d" % hp,
                        waits=[ev_dnT_w])
                    ev_dnT_st[j][kk] = ev_s
                    st_evs.append(ev_s)
                ev_dnT_ok_par[hp] = ev_dnT_w
                ev_dnT_stl_par[hp] = st_evs[-1]
            else:
                guT_store_ev = None
                ev_guT_ok = None
                for kk in range(KT):
                    ev_guT_ok = P.emit("sync", lambda e, j=j, kk=kk, gp=gp: e.dma_start(
                        guT[gp][:, kk, :], scr_guT[j, kk, :, :]), inc="dG%d" % gp,
                        waits=ev_guT_free[gp] + w_totals[0])
                ev_guT_free[gp] = []
                ev_dnT_ok = None
                for kk in range(4):
                    ev_dnT_ok = P.emit("sync", lambda e, j=j, kk=kk, hp=hp: e.dma_start(
                        dnT[hp][:, kk, :], scr_dnT[j, kk, :, :]), inc="dD%d" % hp,
                        waits=[ev_dnT_rd_par[hp]] + w_totals[0])
                ev_dnT_ok_par[hp] = ev_dnT_ok
                ev_dnT_stl_par[hp] = None
            ev_h = None
            ev_dnl = None
            for mi in range(4):
                fb = [acq_f32() for _ in range(4)]
                evb = [None] * 4
                for kk in range(KT):
                    for gu_i in range(2):
                        col0 = 512 * gu_i + mi * 128
                        for th in range(2):
                            bidx = gu_i * 2 + th
                            evb[bidx] = P.emit("tensor",
                                lambda e, kk=kk, col0=col0, th=th, b=fb[bidx], gp=gp: e.matmul(
                                    pf[b][:, :], guT[gp][:, kk, col0:col0 + 128],
                                    xn2T_f[0][:, kk, th * 512:(th + 1) * 512],
                                    start=(kk == 0), stop=(kk == KT - 1)),
                                inc=("sT" if kk == KT - 1 else None),
                                waits=[ev_guT_ok, ev_xf, pf_free[fb[bidx]]])
                ev_gu_last = evb[3]
                for th in range(2):
                    fg_, fu_ = fb[th], fb[2 + th]
                    spp = th
                    ev_sl = P.emit("scalar", lambda e, fg_=fg_, spp=spp: e.activation(
                        sgl[spp][:, :], pf[fg_][:, :], AF.Silu), inc="sS",
                        waits=[evb[th], ev_sgl_rd[spp]])
                    pf_free[fg_] = ev_sl
                    ev_h = P.emit("vector", lambda e, mi=mi, th=th, fu_=fu_, spp=spp, hp=hp: e.tensor_mul(
                        h_b[hp][:, mi, th * 512:(th + 1) * 512], sgl[spp][:, :],
                        pf[fu_][:, :]), inc="sV",
                        waits=[ev_sl, evb[2 + th], ev_h_rd_par[hp]])
                    pf_free[fu_] = ev_h
                    ev_sgl_rd[spp] = ev_h
            ev_h_par[hp] = ev_h
            ev_guT_free[gp] = [ev_gu_last] + ([guT_store_ev] if guT_store_ev else [])
            if j > 0:
                ev_dnl, _ = emit_down_part(j - 1, list(range(NT_F)))
                emit_down_fin(j - 1, ev_dnl)
            if debug and f == 0 and j == 2:
                ygu = nc.dram_tensor("y_gu2", [128, KT * D], BF16,
                                     kind="ExternalOutput")
                ydn = nc.dram_tensor("y_dn2", [128, 4 * D], BF16,
                                     kind="ExternalOutput")
                yh = nc.dram_tensor("y_h2", [128, 4 * CH_F], BF16,
                                    kind="ExternalOutput")
                ev_d1 = P.emit("gpsimd", lambda e: e.dma_start(
                    ygu[:, :], guT[gp][:, :, :].rearrange("p a b -> p (a b)")),
                    inc="dB", waits=[ev_guT_ok])
                ev_d2 = P.emit("gpsimd", lambda e: e.dma_start(
                    ydn[:, :], dnT[hp][:, :, :].rearrange("p a b -> p (a b)")),
                    inc="dB", waits=[ev_dnT_ok_par[hp]])
                ev_d3 = P.emit("gpsimd", lambda e: e.dma_start(
                    yh[:, :], h_b[hp][:, :, :].rearrange("p a b -> p (a b)")),
                    inc="dB", waits=[ev_h])
                ev_guT_free[gp].append(ev_d1)
                ev_dnT_rd_par[hp] = ev_d2
                ev_h_rd_par[hp] = ev_d3
        ev_dnl, ev_acc_done = emit_down_part(NJOB - 1, list(range(NT_F)))
        emit_down_fin(NJOB - 1, ev_dnl)
        ev_xn2Tf_rd[0] = ev_gu_last
        for t in range(NT_F):
            tp = t % 2
            row0 = f * CH_F + t * 128
            ev_lx = P.emit("sync", lambda e, row0=row0, tp=tp: e.dma_start(
                x1f[tp][:, :], scr_x1[row0:row0 + 128, :]), inc="dF%d" % tp,
                waits=[ev_x1_stored[row0 // CH_A], ev_x1f_rd[tp]])
            ev_o = P.emit("vector", lambda e, t=t, tp=tp: e.tensor_add(
                outb[tp][:, :], acc[:, t, :], x1f[tp][:, :]), inc="sV",
                waits=[ev_lx, ev_acc_done, ev_outb_st[tp]])
            ev_x1f_rd[tp] = ev_o
            seq, r = row0 // S, row0 % S
            ev_out_final = P.emit("gpsimd", lambda e, seq=seq, r=r, tp=tp: e.dma_start(
                y_h[seq, r:r + 128, :], outb[tp][:, :]), inc="gO%d" % tp, waits=[ev_o])
            ev_outb_st[tp] = ev_out_final

    if debug:
        yx1 = nc.dram_tensor("y_x1", [T, D], F32, kind="ExternalOutput")
        ycb = nc.dram_tensor("y_comb", [128, NTT * NE], F32, kind="ExternalOutput")
        yxn = nc.dram_tensor("y_xn2T", [KT, 128, T], BF16, kind="ExternalOutput")
        ev_d1 = P.emit("gpsimd", lambda e: e.dma_start(yx1[:, :], scr_x1[:, :]),
                       inc="dB", waits=[ev_out_final] + list(ev_x1_stored))
        ev_d2 = P.emit("gpsimd", lambda e: e.dma_start(ycb[:, :], combine[:, :]),
                       inc="dB")
        ev_d3 = P.emit("gpsimd", lambda e: e.dma_start(
            yxn.rearrange("k p t -> p k t"), scr_xn2T.rearrange("k p t -> p k t")),
            inc="dB", waits=list(ev_xn2T_stored))
        P.wait("gpsimd", ev_d3)
    P.wait("gpsimd", ev_outb_st[0])
    P.wait("gpsimd", ev_outb_st[1])
    f_es.close()

    with es:
        sems = {s: es.enter_context(nc.semaphore(s))
                for s in P.cnt}
        with nc.Block() as block:
            @block.sync
            def _(sync):
                P.run("sync", sync, sems)

            @block.tensor
            def _(tensor):
                P.run("tensor", tensor, sems)

            @block.vector
            def _(vector):
                P.run("vector", vector, sems)

            @block.scalar
            def _(scalar):
                P.run("scalar", scalar, sems)

            @block.gpsimd
            def _(gpsimd):
                P.run("gpsimd", gpsimd, sems)

    return nc


_NC_CACHE = {}


def kernel(L0_attn_x, L0_attn_gamma, L0_attn_wq, L0_attn_wk, L0_attn_wv,
           L0_attn_wa, L0_attn_wg, L0_attn_wo, L0_ffn_gate_weight,
           L0_ffn_e_score_correction_bias, L0_ffn_experts_gate_up,
           L0_ffn_experts_down, L0_ffn_shared_gate_w, L0_ffn_shared_up_w,
           L0_ffn_shared_down_w):
    f = lambda a: np.ascontiguousarray(np.asarray(a, dtype=np.float32))
    x = f(L0_attn_x)
    B, S, _ = x.shape
    n_cores = 8
    nseq = B // n_cores
    key = (nseq, S)
    if key not in _NC_CACHE:
        _NC_CACHE[key] = build_nc(nseq, S)
    nc = _NC_CACHE[key]
    base = {
        "gamma": np.ascontiguousarray(f(L0_attn_gamma).reshape(KT, 128).T),
        "wq": f(L0_attn_wq), "wk": f(L0_attn_wk), "wv": f(L0_attn_wv),
        "wa": f(L0_attn_wa), "wg": f(L0_attn_wg), "wo": f(L0_attn_wo),
        "gate_w": f(L0_ffn_gate_weight),
        "gate_b": f(L0_ffn_e_score_correction_bias).reshape(1, NE),
        "egu": f(L0_ffn_experts_gate_up), "edn": f(L0_ffn_experts_down),
        "sgw": f(L0_ffn_shared_gate_w), "suw": f(L0_ffn_shared_up_w),
        "sdw": f(L0_ffn_shared_down_w),
    }
    in_maps = []
    for ci in range(n_cores):
        m = dict(base)
        m["x"] = np.ascontiguousarray(x[ci * nseq:(ci + 1) * nseq])
        in_maps.append(m)
    res = run_bass_kernel_spmd(nc, in_maps, core_ids=list(range(n_cores)))
    out = np.concatenate([np.asarray(r["y"]) for r in res.results], axis=0)
    return out.astype(np.float32)


# version 14
# speedup vs baseline: 1.1967x; 1.1769x over previous
import numpy as np
import ml_dtypes
from contextlib import ExitStack
import concourse.bass as bass
import concourse.mybir as mybir
from concourse.bass_utils import run_bass_kernel_spmd

F32 = mybir.dt.float32
BF16 = mybir.dt.bfloat16
I32 = mybir.dt.int32
AF = mybir.ActivationFunctionType
ALU = mybir.AluOpType
AX = mybir.AxisListType

D = 1024
KT = 8
NE = 32
NJOB = 34
CH_A = 512
CH_F = 1024
EPS = 1e-5

ENGS = ("sync", "tensor", "vector", "scalar", "gpsimd")
OWN = {"sync": None, "tensor": "sT", "vector": "sV", "scalar": "sS", "gpsimd": None}


class Prog:

    def __init__(self):
        self.q = {e: [] for e in ENGS}
        self.cnt = {}
        self.last_wait = {e: {} for e in ENGS}

    def wait(self, eng, ev, flush=False):
        if ev is None:
            return
        sem, c = ev
        if c <= self.last_wait[eng].get(sem, -1):
            return
        self.last_wait[eng][sem] = c
        self.q[eng].append(("w", sem, c))

    def emit(self, eng, fn, inc=None, waits=()):
        for ev in waits:
            self.wait(eng, ev)
        ev = None
        if inc is not None:
            step = 1 if inc in ("sT", "sV", "sS") else 16
            self.cnt[inc] = self.cnt.get(inc, 0) + step
            ev = (inc, self.cnt[inc])
        self.q[eng].append(("i", fn, inc))
        return ev

    def run(self, eng_name, eng, sems):
        for item in self.q[eng_name]:
            if item[0] == "w":
                eng.wait_ge(sems[item[1]], item[2])
            else:
                inst = item[1](eng)
                if item[2] is not None:
                    inst.then_inc(sems[item[2]],
                                  1 if item[2] in ("sT", "sV", "sS") else 16)


def build_nc(n_seq, S, debug=False, attn_only=False):
    T = n_seq * S
    NCA = T // CH_A
    NCF = T // CH_F
    CPS = S // CH_A
    NTT = T // 128
    NT_F = CH_F // 128

    nc = bass.Bass("TRN2", target_bir_lowering=False)

    x_h = nc.dram_tensor("x", [n_seq, S, D], F32, kind="ExternalInput")
    gamma_h = nc.dram_tensor("gamma", [128, KT], F32, kind="ExternalInput")
    wqkvag_h = [nc.dram_tensor(n, [D, D], F32, kind="ExternalInput")
                for n in ("wq", "wk", "wv", "wa", "wg")]
    wo_h = nc.dram_tensor("wo", [D, D], F32, kind="ExternalInput")
    gw_h = nc.dram_tensor("gate_w", [NE, D], F32, kind="ExternalInput")
    gb_h = nc.dram_tensor("gate_b", [1, NE], F32, kind="ExternalInput")
    egu_h = nc.dram_tensor("egu", [NE, D, D], F32, kind="ExternalInput")
    edn_h = nc.dram_tensor("edn", [NE, D, 512], F32, kind="ExternalInput")
    sgw_h = nc.dram_tensor("sgw", [D, D], F32, kind="ExternalInput")
    suw_h = nc.dram_tensor("suw", [D, D], F32, kind="ExternalInput")
    sdw_h = nc.dram_tensor("sdw", [D, D], F32, kind="ExternalInput")
    y_h = nc.dram_tensor("y", [n_seq, S, D], F32, kind="ExternalOutput")

    scr_x1 = nc.dram_tensor("scr_x1", [T, D], F32)
    scr_xn2T = nc.dram_tensor("scr_xn2T", [KT, 128, T], BF16)
    scr_guT = nc.dram_tensor("scr_guT", [NJOB, KT, 128, D], BF16)
    scr_dnT = nc.dram_tensor("scr_dnT", [NJOB, 4, 128, D], BF16)

    ident_h = nc.inline_tensor(np.eye(128, dtype=ml_dtypes.bfloat16), name="ident")
    ones_h = nc.inline_tensor(np.ones((1, 128), dtype=np.float32), name="ones1")

    def gu_src(j, kk):
        if j >= 2:
            return egu_h[j - 2, kk * 128:(kk + 1) * 128, :]
        half = j
        if kk < 4:
            return sgw_h[half * 512 + kk * 128: half * 512 + (kk + 1) * 128, :]
        return suw_h[half * 512 + (kk - 4) * 128: half * 512 + (kk - 3) * 128, :]

    def dn_src(j, kk):
        if j >= 2:
            return edn_h[j - 2, kk * 128:(kk + 1) * 128, :]
        return sdw_h[kk * 128:(kk + 1) * 128, j * 512:(j + 1) * 512]

    P = Prog()
    es = ExitStack()
    sb = lambda n, s, dt=F32: es.enter_context(nc.sbuf_tensor(n, s, dt))

    gwT_b = sb("gwT_b", [128, KT, NE], BF16)
    gam_s = sb("gam_s", [128, KT])
    bias_bc = sb("bias_bc", [128, NE])
    ones_s = sb("ones_s", [1, 128])
    ident_s = sb("ident_s", [128, 128], BF16)
    zeros2 = sb("zeros2", [128, 512])
    gb_s = sb("gb_s", [1, NE])
    gw_s = sb("gw_s", [NE, D])
    gwb_s = sb("gwb_s", [NE, D], BF16)
    combine = sb("combine", [128, NTT * NE])
    carry = sb("carry", [128, KT])

    pf = [es.enter_context(nc.psum_tensor(f"pf{i}", [128, 512], F32)) for i in range(6)]
    pbt = [es.enter_context(nc.psum_tensor(f"pb{i}", [128, 512], F32)) for i in range(2)]
    pb = [pbt[i][:, :].bitcast(BF16) for i in range(2)]
    pf_free = [None] * 6
    pb_free = [None] * 2
    _nx = {"f": 0, "b": 0}

    def acq_f32():
        i = _nx["f"]; _nx["f"] = (i + 1) % 6
        return i

    def acq_bf():
        i = _nx["b"]; _nx["b"] = (i + 1) % 2
        return i

    a_es = ExitStack()
    sba = lambda n, s, dt=F32: a_es.enter_context(nc.sbuf_tensor(n, s, dt))
    xc = [sba("xc0", [128, 4, D]), sba("xc1", [128, 4, D])]
    x1c = sba("x1c", [128, 4, D])
    ssum = sba("ssum", [128, 4]); ssq = sba("ssq", [128, 4])
    ssum2 = sba("ssum2", [128, 4]); ssq2 = sba("ssq2", [128, 4])
    stA = [sba(f"stA{i}", [128, 4]) for i in range(4)]
    sA = sba("sA", [128, 4]); bA = sba("bA", [128, 4])
    sA2 = sba("sA2", [128, 4]); bA2 = sba("bA2", [128, 4])
    iN = sba("iN", [128, 4], I32)
    sA_all = sba("sA_all", [128, NTT]); bA_all = sba("bA_all", [128, NTT])
    W_b = [sba(f"W{i}", [128, KT, D], BF16) for i in range(5)]
    wo_b = sba("wo_b", [128, KT, D], BF16)
    xn_b = sba("xn_b", [128, 4, D], BF16)
    xnT_b = sba("xnT_b", [128, KT, CH_A], BF16)
    y2T_b = sba("y2T_b", [128, KT, CH_A], BF16)
    xn2T_b = sba("xn2T_b", [128, KT, CH_A], BF16)
    s_k = [sba("s_k0", [128, CH_A])] * 2
    s_q = [sba(f"s_q{p}", [128, CH_A]) for p in range(2)]
    s_sa = [sba("s_sa0", [128, CH_A])] * 2
    s_sg = [sba("s_sg0", [128, CH_A])] * 2
    v_kv = sba("v_kv", [128, CH_A]); v_gs = sba("v_gs", [128, CH_A])
    v_y = sba("v_y", [128, CH_A])
    r_sc = [sba(f"r_sc{i}", [128, NE]) for i in range(4)]
    r_sfc = sba("r_sfc", [128, NE]); r_gmax = sba("r_gmax", [128, 8])
    r_m8 = sba("r_m8", [128, 8]); r_msk = sba("r_msk", [128, NE])
    r_tw = sba("r_tw", [128, NE]); r_dn = sba("r_dn", [128, 1])
    r_fc = sba("r_fc", [128, 1])
    a_es.close()

    st_es = ExitStack()
    stage = st_es.enter_context(nc.sbuf_tensor("stage", [128, KT, D], F32))

    ev_in = []
    ev_in.append(P.emit("sync", lambda e: e.dma_start(ident_s[:, :], ident_h[:, :]), inc="dI"))
    ev_in.append(P.emit("sync", lambda e: e.dma_start(ones_s[:, :], ones_h[:, :]), inc="dI"))
    ev_in.append(P.emit("sync", lambda e: e.dma_start(
        gam_s[:, :], gamma_h[:, :]), inc="dI"))
    ev_in.append(P.emit("sync", lambda e: e.dma_start(gb_s[:, :], gb_h[:, :]), inc="dI"))
    ev_in.append(P.emit("sync", lambda e: e.dma_start(gw_s[:, :], gw_h[:, :]), inc="dI"))
    ev_zero = P.emit("vector", lambda e: e.memset(zeros2[:, :], 0.0), inc="sV")

    ev_casts = []
    ev_stage_rd = None
    for wi, wh in enumerate(wqkvag_h + [wo_h]):
        ld = P.emit("sync", lambda e, wh=wh: e.dma_start(
            stage[:, :, :], wh.rearrange("(k p) n -> p k n", p=128)),
            inc="dW", waits=[ev_stage_rd] + ([ev_in[-1]] if wi == 0 else []))
        dst = W_b[wi] if wi < 5 else wo_b
        last = None
        for kk in range(KT):
            last = P.emit("scalar", lambda e, wi=wi, kk=kk, dst=dst: e.activation(
                dst[:, kk, :], stage[:, kk, :], AF.Identity,
                scale=(gam_s[:, kk:kk + 1] if wi < 5 else 1.0)),
                inc="sS", waits=[ld] + ([ev_in[-1]] if wi == 0 else []))
        ev_stage_rd = last
        ev_casts.append(last)
    st_es.close()

    ev_gwb = P.emit("scalar", lambda e: e.copy(gwb_s[:, :], gw_s[:, :]),
                    inc="sS", waits=[ev_in[-1]])
    bi = acq_bf()
    ev_t = None
    for kk in range(KT):
        ev_t = P.emit("tensor", lambda e, kk=kk, bi=bi: e.transpose(
            pb[bi][:, kk * NE:(kk + 1) * NE],
            gwb_s[:, kk * 128:(kk + 1) * 128], ident_s[0:32, 0:32]),
            inc="sT", waits=[ev_gwb, ev_in[-1], pb_free[bi]])
    ev_gwT = P.emit("vector", lambda e, bi=bi: e.tensor_copy(
        gwT_b[:, :, :], pb[bi][:, 0:KT * NE].rearrange("p (k n) -> p k n", k=KT)),
        inc="sV", waits=[ev_t])
    pb_free[bi] = ev_gwT
    fi = acq_f32()
    ev_bb = P.emit("tensor", lambda e, fi=fi: e.matmul(
        pf[fi][:, 0:NE], ones_s[:, :], gb_s[:, :], start=True, stop=True),
        inc="sT", waits=[ev_in[-1]])
    ev_bias = P.emit("vector", lambda e, fi=fi: e.tensor_copy(
        bias_bc[:, :], pf[fi][:, 0:NE]), inc="sV", waits=[ev_bb])
    pf_free[fi] = ev_bias

    ev_startup = ev_casts + [ev_gwT, ev_bias, ev_zero]

    def vstep(fn, waits=()):
        ev = P.emit("vector", fn, inc="sV", waits=waits)
        P.wait("vector", ev, flush=True)
        return ev

    pSUM = x1c[:, 0, 0:NTT]
    pSQ = x1c[:, 0, NTT:2 * NTT]
    pM = x1c[:, 0, 2 * NTT:3 * NTT]
    pMS = x1c[:, 0, 3 * NTT:4 * NTT]
    pU = x1c[:, 0, 4 * NTT:5 * NTT]
    pT = x1c[:, 0, 5 * NTT:6 * NTT]
    pIf = x1c[:, 0, 6 * NTT:7 * NTT]
    pI = pIf.bitcast(I32)
    ev_pp_free = [None, None]
    ev_pp_sqrd = [None] * 4
    ev_pp_red = None
    for c in range(NCA):
        seq, lc = c // CPS, c % CPS
        xp = c % 2
        ev_px = P.emit("gpsimd", lambda e, seq=seq, lc=lc, xp=xp: e.dma_start(
            xc[xp][:, :, :],
            x_h[seq, lc * CH_A:(lc + 1) * CH_A, :].rearrange("(i p) n -> p i n", p=128)),
            inc="dP%d" % xp,
            waits=[ev_pp_free[xp]] + (ev_startup if c <= 1 else []))
        for i in range(4):
            g = c * 4 + i
            ev_pp_sq = P.emit("scalar", lambda e, i=i, xp=xp: e.activation(
                xn_b[:, i, :], xc[xp][:, i, :], AF.Square), inc="sS",
                waits=[ev_px, ev_pp_sqrd[i]])
            ev_pp_sqrd[i] = P.emit("vector", lambda e, g=g, i=i: e.reduce_sum(
                x1c[:, 0, NTT + g:NTT + g + 1], xn_b[:, i, :], axis=AX.X),
                inc="sV", waits=[ev_pp_sq])
            ev_pp_red = P.emit("vector", lambda e, g=g, i=i, xp=xp: e.reduce_sum(
                x1c[:, 0, g:g + 1], xc[xp][:, i, :], axis=AX.X), inc="sV")
        ev_pp_free[xp] = ev_pp_red
    P.wait("vector", ev_pp_red, flush=True)
    vstep(lambda e: e.tensor_scalar(pM, pSUM, 1.0 / D, None, ALU.mult))
    vstep(lambda e: e.tensor_mul(pMS, pM, pM))
    vstep(lambda e: e.tensor_scalar(pU, pSQ, 1.0 / D, None, ALU.mult))
    vstep(lambda e: e.tensor_sub(pU, pU, pMS))
    vstep(lambda e: e.tensor_scalar(
        pU, pU, float(D) + EPS, 1e-10, ALU.mult, ALU.add))
    vstep(lambda e: e.tensor_scalar(
        pI, pU.bitcast(I32), 1, None, ALU.logical_shift_right))
    vstep(lambda e: e.tensor_scalar(
        pI, pI, 0x5F3759DF, -1, ALU.subtract, ALU.mult))
    for it in range(2):
        vstep(lambda e: e.tensor_mul(pT, pIf, pIf))
        vstep(lambda e: e.tensor_mul(pT, pT, pU))
        vstep(lambda e: e.tensor_scalar(pT, pT, -0.5, 1.5, ALU.mult, ALU.add))
        if it == 0:
            vstep(lambda e: e.tensor_mul(pIf, pIf, pT))
        else:
            vstep(lambda e: e.tensor_mul(pT, pIf, pT))
            vstep(lambda e: e.tensor_scalar(
                sA_all[:, :], pT, float(np.sqrt(D)), None, ALU.mult))
    ev_lnA = vstep(lambda e: e.scalar_tensor_tensor(
        bA_all[:, :], pM, -1.0, sA_all[:, :], ALU.mult, ALU.mult))

    def emit_ln(xsrc, ssum_t, ssq_t, sA_t, bA_t, attn_mode, waits_s, waits_v):
        for i in range(4):
            ev_sq = P.emit("scalar", lambda e, i=i: e.activation(
                xn_b[:, i, :], xsrc[:, i, :], AF.Square), inc="sS",
                waits=list(waits_s) if i == 0 else ())
            P.emit("vector", lambda e, i=i: e.reduce_sum(
                ssq_t[:, i:i + 1], xn_b[:, i, :], axis=AX.X),
                inc="sV", waits=[ev_sq] + (list(waits_v) if i == 0 else []))
            ev = P.emit("vector", lambda e, i=i: e.reduce_sum(
                ssum_t[:, i:i + 1], xsrc[:, i, :], axis=AX.X), inc="sV")
        P.wait("vector", ev, flush=True)
        m, msq, u, t = stA
        vstep(lambda e: e.tensor_scalar(
            m[:, :], ssum_t[:, :], 1.0 / D, None, ALU.mult))
        vstep(lambda e: e.tensor_mul(msq[:, :], m[:, :], m[:, :]))
        vstep(lambda e: e.tensor_scalar(
            u[:, :], ssq_t[:, :], 1.0 / D, None, ALU.mult))
        vstep(lambda e: e.tensor_sub(u[:, :], u[:, :], msq[:, :]))
        if attn_mode:
            vstep(lambda e: e.tensor_scalar(
                u[:, :], u[:, :], float(D) + EPS, 1e-10, ALU.mult, ALU.add))
        else:
            vstep(lambda e: e.tensor_scalar(
                u[:, :], u[:, :], 1.0, EPS, ALU.mult, ALU.add))
        vstep(lambda e: e.tensor_scalar(
            iN[:, :], u[:, :].bitcast(I32), 1, None, ALU.logical_shift_right))
        vstep(lambda e: e.tensor_scalar(
            iN[:, :], iN[:, :], 0x5F3759DF, -1, ALU.subtract, ALU.mult))
        y = iN[:, :].bitcast(F32)
        NIT = 2
        for it in range(NIT):
            vstep(lambda e: e.tensor_mul(t[:, :], y, y))
            vstep(lambda e: e.tensor_mul(t[:, :], t[:, :], u[:, :]))
            vstep(lambda e: e.tensor_scalar(
                t[:, :], t[:, :], -0.5, 1.5, ALU.mult, ALU.add))
            if it < NIT - 1:
                vstep(lambda e: e.tensor_mul(y, y, t[:, :]))
            elif attn_mode:
                vstep(lambda e: e.tensor_mul(t[:, :], y, t[:, :]))
                vstep(lambda e: e.tensor_scalar(
                    sA_t[:, :], t[:, :], float(np.sqrt(D)), None, ALU.mult))
            else:
                vstep(lambda e: e.tensor_mul(sA_t[:, :], y, t[:, :]))
        return vstep(lambda e: e.scalar_tensor_tensor(
            bA_t[:, :], m[:, :], -1.0, sA_t[:, :], ALU.mult, ALU.mult))

    def emit_transpose4(src_b, dstT, ev_src, dst_free_evs):
        evs = []
        ev_tt = None
        for i in range(4):
            bi = acq_bf()
            for ch in range(KT):
                ev_tt = P.emit("tensor", lambda e, i=i, ch=ch, bi=bi: e.transpose(
                    pb[bi][:, ch * 128:(ch + 1) * 128],
                    src_b[:, i, ch * 128:(ch + 1) * 128], ident_s[:, :]),
                    inc=("sT" if ch == KT - 1 else None),
                    waits=[ev_src, pb_free[bi]])
            ev_ev = P.emit("vector", lambda e, i=i, bi=bi: e.tensor_copy(
                dstT[:, :, i * 128:(i + 1) * 128],
                pb[bi][:, :].rearrange("p (k n) -> p k n", k=KT)),
                inc="sV", waits=[ev_tt] + list(dst_free_evs))
            pb_free[bi] = ev_ev
            evs.append(ev_ev)
        return evs, ev_tt

    ev_xc_free = [ev_pp_free[0], ev_pp_free[1]]
    ev_xnT_rd = [None]
    ev_y2_rd = [None]
    ev_x1_free = [[]]
    ev_xn2T_free = [[]]
    ev_sAbA_rd = [None]
    ev_sAbA2_rd = [None]
    ev_stats_rd = [None]
    ev_stats2_rd = [None]
    ev_scr_rd = {k: [None, None] for k in ("k", "q", "sa", "sg")}
    ev_tw = [None] * 4
    ev_x1_stored = [None] * NCA
    ev_xn2T_stored = [None] * NCA
    ev_comb = [None] * NTT
    tt_done = [None] * NCA
    tt2_done = [None] * NCA
    ev_y2_prev = [None]
    ev_t_prev = [None]
    ev_carry_m = [None] * KT
    ST = [{} for _ in range(NCA)]

    def attn_front(c):
        seq, lc = c // CPS, c % CPS
        xp = c % 2
        ev_x = P.emit("sync", lambda e, seq=seq, lc=lc, xp=xp: e.dma_start(
            xc[xp][:, :, :],
            x_h[seq, lc * CH_A:(lc + 1) * CH_A, :].rearrange("(i p) n -> p i n", p=128)),
            inc="dX%d" % xp, waits=[ev_xc_free[xp]] + (ev_startup if c <= 1 else []))
        trash_free = (tt_done[c - 1] if attn_only else
                      (tt2_done[c - 2] if c >= 2 else None)) if c >= 1 else None
        ev_xn = None
        for i in range(4):
            g = c * 4 + i
            ev_xn = P.emit("scalar", lambda e, i=i, g=g, xp=xp: e.activation(
                xn_b[:, i, :], xc[xp][:, i, :], AF.Identity,
                scale=sA_all[:, g:g + 1], bias=bA_all[:, g:g + 1]),
                inc="sS", waits=[ev_x, trash_free, ev_lnA] if i == 0 else ())
        ev_xc_free[xp] = ev_xn
        evs, ev_tt = emit_transpose4(xn_b, xnT_b, ev_xn, [ev_xnT_rd[0]])
        tt_done[c] = ev_tt
        ST[c]["ev_xnT"] = evs[-1]

    def attn_backA(c):
        seq, lc = c // CPS, c % CPS
        ev_xnT = ST[c]["ev_xnT"]
        ev_y2 = None
        ev_last_qkvag = None
        for m in range(KT):
            mpar = m % 2
            banks = []
            for wi in range(5):
                fi2 = acq_f32()
                ev_mm = None
                for kk in range(KT):
                    ev_mm = P.emit("tensor", lambda e, wi=wi, kk=kk, m=m, fi2=fi2: e.matmul(
                        pf[fi2][:, :], W_b[wi][:, kk, m * 128:(m + 1) * 128],
                        xnT_b[:, kk, :], start=(kk == 0), stop=(kk == KT - 1)),
                        inc=("sT" if kk == KT - 1 else None),
                        waits=[ev_xnT, pf_free[fi2]])
                banks.append((fi2, ev_mm))
            ev_last_qkvag = banks[-1][1]
            (fq, evq), (fk, evk), (fv, evv), (fa, eva), (fg, evg) = banks
            ev_ek = P.emit("scalar", lambda e, fk=fk, mpar=mpar: e.copy(
                s_k[mpar][:, :], pf[fk][:, :]), inc="sS",
                waits=[evk, ev_scr_rd["k"][mpar]])
            ev_eq = P.emit("scalar", lambda e, fq=fq, mpar=mpar: e.copy(
                s_q[mpar][:, :], pf[fq][:, :]), inc="sS",
                waits=[evq, ev_scr_rd["q"][mpar]])
            ev_ea = P.emit("scalar", lambda e, fa=fa, mpar=mpar: e.activation(
                s_sa[mpar][:, :], pf[fa][:, :], AF.Sigmoid), inc="sS",
                waits=[eva, ev_scr_rd["sa"][mpar]])
            ev_eg = P.emit("scalar", lambda e, fg=fg, mpar=mpar: e.activation(
                s_sg[mpar][:, :], pf[fg][:, :], AF.Sigmoid), inc="sS",
                waits=[evg, ev_scr_rd["sg"][mpar]])
            pf_free[fk] = ev_ek
            pf_free[fq] = ev_eq
            pf_free[fa] = ev_ea
            ev_kv = P.emit("vector", lambda e, fv=fv, mpar=mpar: e.tensor_mul(
                v_kv[:, :], s_k[mpar][:, :], pf[fv][:, :]), inc="sV",
                waits=[ev_ek, evv, ev_y2_prev[0]])
            pf_free[fv] = ev_kv
            ev_gs = P.emit("vector", lambda e, fg=fg, mpar=mpar: e.tensor_mul(
                v_gs[:, :], s_sg[mpar][:, :], pf[fg][:, :]), inc="sV",
                waits=[ev_eg, ev_y2_prev[0]])
            pf_free[fg] = ev_gs
            init = 0.0 if lc == 0 else carry[:, m:m + 1]
            ev_scan = P.emit("vector", lambda e, mpar=mpar, init=init: e.tensor_tensor_scan(
                v_y[:, :], s_sa[mpar][:, :], v_kv[:, :], init, ALU.mult, ALU.add),
                inc="sV", waits=[ev_ea, ev_kv, ev_t_prev[0], ev_carry_m[m]])
            P.wait("vector", ev_scan, flush=True)
            ev_carry_m[m] = P.emit("vector", lambda e, m=m: e.tensor_copy(
                carry[:, m:m + 1], v_y[:, CH_A - 1:CH_A]), inc="sV")
            ev_t = P.emit("vector", lambda e, mpar=mpar: e.tensor_mul(
                v_kv[:, :], s_q[mpar][:, :], v_y[:, :]), inc="sV", waits=[ev_eq])
            ev_y2 = P.emit("vector", lambda e, m=m: e.tensor_mul(
                y2T_b[:, m, :], v_kv[:, :], v_gs[:, :]), inc="sV",
                waits=[ev_y2_rd[0], ev_t])
            ev_t_prev[0] = ev_t
            ev_y2_prev[0] = ev_y2
            ev_scr_rd["k"][0] = ev_scr_rd["k"][1] = ev_kv
            ev_scr_rd["q"][mpar] = ev_y2
            ev_scr_rd["sa"][0] = ev_scr_rd["sa"][1] = ev_scan
            ev_scr_rd["sg"][0] = ev_scr_rd["sg"][1] = ev_gs
        ev_xnT_rd[0] = ev_last_qkvag
        ev_res = None
        evR = None
        for i in range(4):
            fL = acq_f32(); fR = acq_f32()
            evL = None
            for kk in range(KT):
                evL = P.emit("tensor", lambda e, i=i, kk=kk, fL=fL: e.matmul(
                    pf[fL][:, :], y2T_b[:, kk, i * 128:(i + 1) * 128],
                    wo_b[:, kk, 0:512], start=(kk == 0), stop=(kk == KT - 1)),
                    inc=("sT" if kk == KT - 1 else None),
                    waits=[ev_y2, pf_free[fL]])
                evR = P.emit("tensor", lambda e, i=i, kk=kk, fR=fR: e.matmul(
                    pf[fR][:, :], y2T_b[:, kk, i * 128:(i + 1) * 128],
                    wo_b[:, kk, 512:1024], start=(kk == 0), stop=(kk == KT - 1)),
                    inc=("sT" if kk == KT - 1 else None),
                    waits=[pf_free[fR]])
            P.emit("vector", lambda e, i=i, fL=fL, xp=c % 2: e.tensor_add(
                x1c[:, i, 0:512], xc[xp][:, i, 0:512], pf[fL][:, :]),
                waits=[evL] + ev_x1_free[0])
            ev_res = P.emit("vector", lambda e, i=i, fR=fR, xp=c % 2: e.tensor_add(
                x1c[:, i, 512:1024], xc[xp][:, i, 512:1024], pf[fR][:, :]),
                inc="sV", waits=[evR])
            pf_free[fL] = ev_res
            pf_free[fR] = ev_res
        ev_y2_rd[0] = evR
        ev_xc_free[c % 2] = ev_res
        ev_x1_stored[c] = P.emit("gpsimd", lambda e, c=c: e.dma_start(
            scr_x1[c * CH_A:(c + 1) * CH_A, :].rearrange("(i p) n -> p i n", p=128),
            x1c[:, :, :]), inc="gX", waits=[ev_res])
        ev_x1_free[0] = [ev_x1_stored[c]]
        ST[c]["ev_res"] = ev_res

    def attn_backB_pre(c):
        ev_res = ST[c]["ev_res"]
        tt_next = tt_done[c + 1] if c + 1 < NCA else tt_done[c]
        ev_snb2 = emit_ln(x1c, ssum2, ssq2, sA2, bA2, False,
                          [ev_res, tt_next],
                          [ev_stats2_rd[0], ev_sAbA2_rd[0]])
        ev_stats2_rd[0] = ev_snb2
        ev_xn2 = None
        for i in range(4):
            ev_xn2 = P.emit("scalar", lambda e, i=i: e.activation(
                xn_b[:, i, :], x1c[:, i, :], AF.Identity,
                scale=sA2[:, i:i + 1], bias=bA2[:, i:i + 1]),
                inc="sS", waits=[ev_snb2])
        ev_sAbA2_rd[0] = ev_xn2
        ev_x1_free[0].append(ev_xn2)
        ST[c]["ev_xn2"] = ev_xn2

    def attn_backB_post(c):
        ev_xn2 = ST[c]["ev_xn2"]
        evs2, ev_tt2 = emit_transpose4(xn_b, xn2T_b, ev_xn2, ev_xn2T_free[0])
        tt2_done[c] = ev_tt2
        ev_xn2T = evs2[-1]
        ev_st2 = None
        for kk in range(KT):
            ev_st2 = P.emit("gpsimd", lambda e, kk=kk, c=c: e.dma_start(
                scr_xn2T[kk, :, c * CH_A:(c + 1) * CH_A], xn2T_b[:, kk, :]),
                inc="gT", waits=[ev_xn2T])
        ev_xn2T_stored[c] = ev_st2
        ev_xn2T_free[0] = [ev_st2]
        for i in range(4):
            g = c * 4 + i
            fi3 = acq_f32()
            ev_lg = None
            for kk in range(KT):
                ev_lg = P.emit("tensor", lambda e, i=i, kk=kk, fi3=fi3: e.matmul(
                    pf[fi3][:, 0:NE], xn2T_b[:, kk, i * 128:(i + 1) * 128],
                    gwT_b[:, kk, :], start=(kk == 0), stop=(kk == KT - 1)),
                    inc=("sT" if kk == KT - 1 else None),
                    waits=[ev_xn2T, pf_free[fi3]])
            ev_xn2T_free[0].append(ev_lg)
            ev_sc = P.emit("scalar", lambda e, i=i, fi3=fi3: e.activation(
                r_sc[i][:, :], pf[fi3][:, 0:NE], AF.Sigmoid), inc="sS",
                waits=[ev_lg, ev_tw[i]])
            pf_free[fi3] = ev_sc

            def vfl(ev):
                P.wait("vector", ev, flush=True)
            ev1 = P.emit("vector", lambda e, i=i: e.tensor_add(
                r_sfc[:, :], r_sc[i][:, :], bias_bc[:, :]), inc="sV", waits=[ev_sc])
            vfl(ev1)
            ev2 = P.emit("vector", lambda e: e.reduce_max(
                r_gmax[:, :], r_sfc[:, :].rearrange("p (g r) -> p g r", r=4),
                axis=AX.X), inc="sV")
            vfl(ev2)
            ev3 = P.emit("vector", lambda e: e.max(r_m8[:, :], r_gmax[:, :]), inc="sV")
            vfl(ev3)
            ev4 = P.emit("vector", lambda e: e.scalar_tensor_tensor(
                r_msk[:, :].rearrange("p (g r) -> p g r", r=4),
                r_gmax[:, :].rearrange("p (g r) -> p g r", r=1).to_broadcast([128, 8, 4]),
                r_m8[:, 3:4],
                r_sfc[:, :].rearrange("p (g r) -> p g r", r=4),
                ALU.is_ge, ALU.mult), inc="sV")
            vfl(ev4)
            ev5 = P.emit("vector", lambda e: e.max(r_m8[:, :], r_msk[:, :]), inc="sV")
            vfl(ev5)
            ev6 = P.emit("vector", lambda e, i=i: e.scalar_tensor_tensor(
                r_tw[:, :], r_msk[:, :], r_m8[:, 5:6], r_sc[i][:, :],
                ALU.is_ge, ALU.mult), inc="sV")
            ev_tw[i] = ev6
            vfl(ev6)
            ev6b = P.emit("vector", lambda e: e.reduce_sum(
                r_dn[:, :], r_tw[:, :], axis=AX.X), inc="sV")
            vfl(ev6b)
            ev6c = P.emit("vector", lambda e: e.tensor_scalar(
                r_dn[:, :], r_dn[:, :], 1e-20, None, ALU.add), inc="sV")
            vfl(ev6c)
            ev7 = P.emit("vector", lambda e: e.reciprocal(r_fc[:, :], r_dn[:, :]), inc="sV")
            vfl(ev7)
            ev8 = P.emit("vector", lambda e: e.tensor_scalar(
                r_fc[:, :], r_fc[:, :], 2.5, None, ALU.mult), inc="sV")
            vfl(ev8)
            ev_comb[g] = P.emit("vector", lambda e, g=g: e.tensor_tensor(
                combine[:, g * NE:(g + 1) * NE], r_tw[:, :],
                r_fc[:, :].to_broadcast([128, NE]), ALU.mult), inc="sV")

    if attn_only:
        attn_front(0)
        for c in range(NCA):
            attn_backA(c)
            if c + 1 < NCA:
                attn_front(c + 1)
    else:
        attn_front(0)
        for c in range(NCA):
            attn_backA(c)
            if c + 1 < NCA:
                attn_front(c + 1)
            attn_backB_pre(c)
            attn_backB_post(c)

    if attn_only:
        dump = {
            "d_x1": (scr_x1[:, :], [T, D], F32, [ev_x1_stored[-1]]),
            "d_sA": (sA[:, :], [128, 4], F32, []),
            "d_bA": (bA[:, :], [128, 4], F32, []),
            "d_xnb": (xn_b[:, :, :], [128, 4 * D], BF16, []),
            "d_xnT": (xnT_b[:, :, :], [128, KT * CH_A], BF16, []),
            "d_y2T": (y2T_b[:, :, :], [128, KT * CH_A], BF16, []),
            "d_sa": (s_sa[1][:, :], [128, CH_A], F32, []),
            "d_sk": (s_k[1][:, :], [128, CH_A], F32, []),
            "d_sq": (s_q[1][:, :], [128, CH_A], F32, []),
            "d_sg": (s_sg[1][:, :], [128, CH_A], F32, []),
            "d_vy": (v_y[:, :], [128, CH_A], F32, []),
        }
        ev_last_res = ST[NCA - 1]["ev_res"]
        ev_d = None
        for name, (src, shp, dt, extra) in dump.items():
            oh = nc.dram_tensor(name, shp, dt, kind="ExternalOutput")
            src2 = src if len(src.shape) <= 2 else src.rearrange("p a b -> p (a b)")
            ev_d = P.emit("gpsimd", lambda e, oh=oh, src2=src2: e.dma_start(
                oh[:, :] if len(oh.shape) == 2 else oh, src2),
                inc="dB", waits=[ev_last_res] + extra)
        P.wait("gpsimd", ev_d)
        f_es0 = ExitStack()
        with es:
            sems = {s: es.enter_context(nc.semaphore(s))
                    for s in P.cnt}
            with nc.Block() as block:
                @block.sync
                def _(sync):
                    P.run("sync", sync, sems)

                @block.tensor
                def _(tensor):
                    P.run("tensor", tensor, sems)

                @block.vector
                def _(vector):
                    P.run("vector", vector, sems)

                @block.scalar
                def _(scalar):
                    P.run("scalar", scalar, sems)

                @block.gpsimd
                def _(gpsimd):
                    P.run("gpsimd", gpsimd, sems)
        return nc

    attn_done = [(s, c) for s, c in P.cnt.items() if c > 0]

    f_es = ExitStack()
    sbf = lambda n, s, dt=F32: f_es.enter_context(nc.sbuf_tensor(n, s, dt))
    xn2T_f = [sbf(f"xn2T_f{p}", [128, KT, CH_F], BF16) for p in range(2)]
    guT = [sbf(f"guT{p}", [128, KT, D], BF16) for p in range(2)]
    dnT = [sbf(f"dnT{p}", [128, 4, D], BF16) for p in range(2)]
    h_b = [sbf(f"h_b{p}", [128, 4, CH_F], BF16) for p in range(2)]
    sgl = [sbf(f"sgl{p}", [128, 512]) for p in range(2)]
    acc = sbf("acc", [128, NT_F, D])
    x1f = [sbf(f"x1f{p}", [128, D]) for p in range(2)]
    outb = [sbf(f"outb{p}", [128, D]) for p in range(2)]
    NSTG = 6
    stgF = [sbf(f"stgF{p}", [128, D]) for p in range(NSTG)]
    gub = [sbf(f"gub{p}", [128, D], BF16) for p in range(NSTG)]

    for e in ENGS:
        for ev in attn_done:
            if ev[1] > 0:
                P.wait(e, ev)

    ev_guT_free = [[], []]
    ev_h_par = [None, None]
    ev_h_rd_par = [None, None]
    ev_dnT_ok_par = [None, None]
    ev_dnT_rd_par = [None, None]
    ev_dnT_stl_par = [None, None]
    ev_sgl_rd = [None, None]
    ev_xn2Tf_rd = [None, None]
    ev_x1f_rd = [None, None]
    ev_outb_st = [None, None]
    ev_stg_rd = [None] * NSTG
    ev_gub_rd = [None] * NSTG
    stg_ctr = [0]
    ev_guT_st = [[None] * KT for _ in range(NJOB)]
    ev_dnT_st = [[None] * 4 for _ in range(NJOB)]
    ev_out_final = None
    acc_prevL = [None] * NT_F
    acc_prevR = [None] * NT_F

    w_totals = [[]]
    for f in range(NCF):
        if f == 1:
            w_totals[0] = [(s, P.cnt[s]) for s in
                           ("gW0", "gW1", "gW2", "gV0", "gV1") if s in P.cnt]
        need = [ev_xn2T_stored[c]
                for c in range(f * (CH_F // CH_A), (f + 1) * (CH_F // CH_A))]
        ev_xf = None
        for kk in range(KT):
            ev_xf = P.emit("sync", lambda e, kk=kk, f=f: e.dma_start(
                xn2T_f[0][:, kk, :], scr_xn2T[kk, :, f * CH_F:(f + 1) * CH_F]),
                inc="dN0", waits=need + [ev_xn2Tf_rd[0]])
        if debug and f == 0:
            yxnf = nc.dram_tensor("y_xnf", [128, KT * CH_F], BF16,
                                  kind="ExternalOutput")
            P.emit("gpsimd", lambda e: e.dma_start(
                yxnf[:, :], xn2T_f[0][:, :, :].rearrange("p a b -> p (a b)")),
                inc="dB", waits=[ev_xf])
        ev_acc_done = None
        ev_gu_last = None

        def emit_down_part(jd, tlist, f=f):
            hp = jd % 2
            ev_dn_last = None
            ev_last_stt = None
            for t in tlist:
                g = f * NT_F + t
                fL = acq_f32(); fR = acq_f32()
                evL = evR2 = None
                for mi in range(4):
                    evL = P.emit("tensor", lambda e, t=t, mi=mi, fL=fL, hp=hp: e.matmul(
                        pf[fL][:, :], h_b[hp][:, mi, t * 128:(t + 1) * 128],
                        dnT[hp][:, mi, 0:512], start=(mi == 0), stop=(mi == 3)),
                        inc=("sT" if mi == 3 else None),
                        waits=[ev_h_par[hp], ev_dnT_ok_par[hp], pf_free[fL]])
                    evR2 = P.emit("tensor", lambda e, t=t, mi=mi, fR=fR, hp=hp: e.matmul(
                        pf[fR][:, :], h_b[hp][:, mi, t * 128:(t + 1) * 128],
                        dnT[hp][:, mi, 512:1024], start=(mi == 0), stop=(mi == 3)),
                        inc=("sT" if mi == 3 else None),
                        waits=[pf_free[fR]])
                ev_dn_last = evR2
                if jd == 0:
                    mkL = lambda e, t=t, fL=fL: e.scalar_tensor_tensor(
                        acc[:, t, 0:512], pf[fL][:, :], 1.0, zeros2[:, :],
                        ALU.mult, ALU.add)
                    mkR = lambda e, t=t, fR=fR: e.scalar_tensor_tensor(
                        acc[:, t, 512:1024], pf[fR][:, :], 1.0, zeros2[:, :],
                        ALU.mult, ALU.add)
                else:
                    cap = (1.0 if jd == 1 else
                           combine[:, g * NE + (jd - 2): g * NE + (jd - 2) + 1])
                    mkL = lambda e, t=t, fL=fL, cap=cap: e.scalar_tensor_tensor(
                        acc[:, t, 0:512], pf[fL][:, :], cap, acc[:, t, 0:512],
                        ALU.mult, ALU.add)
                    mkR = lambda e, t=t, fR=fR, cap=cap: e.scalar_tensor_tensor(
                        acc[:, t, 512:1024], pf[fR][:, :], cap, acc[:, t, 512:1024],
                        ALU.mult, ALU.add)
                evaL = P.emit("vector", mkL, inc="sV",
                              waits=[evL, ev_comb[g], acc_prevL[t]])
                evaR = P.emit("vector", mkR, inc="sV",
                              waits=[evR2, acc_prevR[t]])
                acc_prevL[t] = evaL
                acc_prevR[t] = evaR
                pf_free[fL] = evaL
                pf_free[fR] = evaR
                ev_last_stt = evaR
            return ev_dn_last, ev_last_stt

        def emit_down_fin(jd, ev_dn_last):
            hp = jd % 2
            ev_h_rd_par[hp] = ev_dn_last
            ev_dnT_rd_par[hp] = ev_dn_last

        for j in range(NJOB):
            gp = j % 2
            hp = j % 2
            if f == 0:
                ev_guT_w = None
                for kk in range(KT):
                    sp = stg_ctr[0] % NSTG
                    stg_ctr[0] += 1
                    ev_ld = P.emit("sync", lambda e, j=j, kk=kk, sp=sp: e.dma_start(
                        stgF[sp][:, :], gu_src(j, kk)), inc="dS%d" % sp,
                        waits=[ev_stg_rd[sp]])
                    ev_cast = P.emit("scalar", lambda e, kk=kk, sp=sp: e.copy(
                        gub[sp][:, :], stgF[sp][:, :]), inc="sS",
                        waits=[ev_ld, ev_gub_rd[sp]])
                    ev_stg_rd[sp] = ev_cast
                    bi2 = acq_bf()
                    ev_tt = None
                    for ib in range(KT):
                        ev_tt = P.emit("tensor", lambda e, ib=ib, bi2=bi2, sp=sp: e.transpose(
                            pb[bi2][:, ib * 128:(ib + 1) * 128],
                            gub[sp][:, ib * 128:(ib + 1) * 128], ident_s[:, :]),
                            inc=("sT" if ib == KT - 1 else None),
                            waits=[ev_cast, pb_free[bi2]])
                    ev_gub_rd[sp] = ev_tt
                    ev_guT_w = P.emit("vector", lambda e, kk=kk, bi2=bi2, gp=gp: e.tensor_copy(
                        guT[gp][:, :, kk * 128:(kk + 1) * 128],
                        pb[bi2][:, :].rearrange("p (k n) -> p k n", k=KT)),
                        inc="sV", waits=[ev_tt] + ev_guT_free[gp])
                    pb_free[bi2] = ev_guT_w
                st_evs = []
                for kk in range(KT):
                    ev_s = P.emit("gpsimd", lambda e, j=j, kk=kk, gp=gp: e.dma_start(
                        scr_guT[j, kk, :, :], guT[gp][:, kk, :]), inc="gW%d" % gp,
                        waits=[ev_guT_w])
                    ev_guT_st[j][kk] = ev_s
                    st_evs.append(ev_s)
                ev_guT_ok = ev_guT_w
                guT_store_ev = st_evs[-1]
                ev_dnT_w = None
                for kk in range(KT):
                    sp = stg_ctr[0] % NSTG
                    stg_ctr[0] += 1
                    ev_ld = P.emit("sync", lambda e, j=j, kk=kk, sp=sp: e.dma_start(
                        stgF[sp][:, 0:512], dn_src(j, kk)), inc="dS%d" % sp,
                        waits=[ev_stg_rd[sp]])
                    ev_cast = P.emit("scalar", lambda e, kk=kk, sp=sp: e.copy(
                        gub[sp][:, 0:512], stgF[sp][:, 0:512]), inc="sS",
                        waits=[ev_ld, ev_gub_rd[sp]])
                    ev_stg_rd[sp] = ev_cast
                    bi2 = acq_bf()
                    ev_tt = None
                    for ib in range(4):
                        ev_tt = P.emit("tensor", lambda e, ib=ib, bi2=bi2, sp=sp: e.transpose(
                            pb[bi2][:, ib * 128:(ib + 1) * 128],
                            gub[sp][:, ib * 128:(ib + 1) * 128], ident_s[:, :]),
                            inc=("sT" if ib == 3 else None),
                            waits=[ev_cast, pb_free[bi2]])
                    ev_gub_rd[sp] = ev_tt
                    ev_dnT_w = P.emit("vector", lambda e, kk=kk, bi2=bi2, hp=hp: e.tensor_copy(
                        dnT[hp][:, :, kk * 128:(kk + 1) * 128],
                        pb[bi2][:, 0:512].rearrange("p (k n) -> p k n", k=4)),
                        inc="sV",
                        waits=[ev_tt, ev_dnT_rd_par[hp], ev_dnT_stl_par[hp]])
                    pb_free[bi2] = ev_dnT_w
                st_evs = []
                for kk in range(4):
                    ev_s = P.emit("gpsimd", lambda e, j=j, kk=kk, hp=hp: e.dma_start(
                        scr_dnT[j, kk, :, :], dnT[hp][:, kk, :]), inc="gV%d" % hp,
                        waits=[ev_dnT_w])
                    ev_dnT_st[j][kk] = ev_s
                    st_evs.append(ev_s)
                ev_dnT_ok_par[hp] = ev_dnT_w
                ev_dnT_stl_par[hp] = st_evs[-1]
            else:
                guT_store_ev = None
                ev_guT_ok = None
                for kk in range(KT):
                    ev_guT_ok = P.emit("sync", lambda e, j=j, kk=kk, gp=gp: e.dma_start(
                        guT[gp][:, kk, :], scr_guT[j, kk, :, :]), inc="dG%d" % gp,
                        waits=ev_guT_free[gp] + w_totals[0])
                ev_guT_free[gp] = []
                ev_dnT_ok = None
                for kk in range(4):
                    ev_dnT_ok = P.emit("sync", lambda e, j=j, kk=kk, hp=hp: e.dma_start(
                        dnT[hp][:, kk, :], scr_dnT[j, kk, :, :]), inc="dD%d" % hp,
                        waits=[ev_dnT_rd_par[hp]] + w_totals[0])
                ev_dnT_ok_par[hp] = ev_dnT_ok
                ev_dnT_stl_par[hp] = None
            ev_h = None
            ev_dnl = None
            for mi in range(4):
                fb = [acq_f32() for _ in range(4)]
                evb = [None] * 4
                for kk in range(KT):
                    for gu_i in range(2):
                        col0 = 512 * gu_i + mi * 128
                        for th in range(2):
                            bidx = gu_i * 2 + th
                            evb[bidx] = P.emit("tensor",
                                lambda e, kk=kk, col0=col0, th=th, b=fb[bidx], gp=gp: e.matmul(
                                    pf[b][:, :], guT[gp][:, kk, col0:col0 + 128],
                                    xn2T_f[0][:, kk, th * 512:(th + 1) * 512],
                                    start=(kk == 0), stop=(kk == KT - 1)),
                                inc=("sT" if kk == KT - 1 else None),
                                waits=[ev_guT_ok, ev_xf, pf_free[fb[bidx]]])
                ev_gu_last = evb[3]
                for th in range(2):
                    fg_, fu_ = fb[th], fb[2 + th]
                    spp = th
                    ev_sl = P.emit("scalar", lambda e, fg_=fg_, spp=spp: e.activation(
                        sgl[spp][:, :], pf[fg_][:, :], AF.Silu), inc="sS",
                        waits=[evb[th], ev_sgl_rd[spp]])
                    pf_free[fg_] = ev_sl
                    ev_h = P.emit("vector", lambda e, mi=mi, th=th, fu_=fu_, spp=spp, hp=hp: e.tensor_mul(
                        h_b[hp][:, mi, th * 512:(th + 1) * 512], sgl[spp][:, :],
                        pf[fu_][:, :]), inc="sV",
                        waits=[ev_sl, evb[2 + th], ev_h_rd_par[hp]])
                    pf_free[fu_] = ev_h
                    ev_sgl_rd[spp] = ev_h
            ev_h_par[hp] = ev_h
            ev_guT_free[gp] = [ev_gu_last] + ([guT_store_ev] if guT_store_ev else [])
            if j > 0:
                ev_dnl, _ = emit_down_part(j - 1, list(range(NT_F)))
                emit_down_fin(j - 1, ev_dnl)
            if debug and f == 0 and j == 2:
                ygu = nc.dram_tensor("y_gu2", [128, KT * D], BF16,
                                     kind="ExternalOutput")
                ydn = nc.dram_tensor("y_dn2", [128, 4 * D], BF16,
                                     kind="ExternalOutput")
                yh = nc.dram_tensor("y_h2", [128, 4 * CH_F], BF16,
                                    kind="ExternalOutput")
                ev_d1 = P.emit("gpsimd", lambda e: e.dma_start(
                    ygu[:, :], guT[gp][:, :, :].rearrange("p a b -> p (a b)")),
                    inc="dB", waits=[ev_guT_ok])
                ev_d2 = P.emit("gpsimd", lambda e: e.dma_start(
                    ydn[:, :], dnT[hp][:, :, :].rearrange("p a b -> p (a b)")),
                    inc="dB", waits=[ev_dnT_ok_par[hp]])
                ev_d3 = P.emit("gpsimd", lambda e: e.dma_start(
                    yh[:, :], h_b[hp][:, :, :].rearrange("p a b -> p (a b)")),
                    inc="dB", waits=[ev_h])
                ev_guT_free[gp].append(ev_d1)
                ev_dnT_rd_par[hp] = ev_d2
                ev_h_rd_par[hp] = ev_d3
        ev_dnl, ev_acc_done = emit_down_part(NJOB - 1, list(range(NT_F)))
        emit_down_fin(NJOB - 1, ev_dnl)
        ev_xn2Tf_rd[0] = ev_gu_last
        for t in range(NT_F):
            tp = t % 2
            row0 = f * CH_F + t * 128
            ev_lx = P.emit("sync", lambda e, row0=row0, tp=tp: e.dma_start(
                x1f[tp][:, :], scr_x1[row0:row0 + 128, :]), inc="dF%d" % tp,
                waits=[ev_x1_stored[row0 // CH_A], ev_x1f_rd[tp]])
            ev_o = P.emit("vector", lambda e, t=t, tp=tp: e.tensor_add(
                outb[tp][:, :], acc[:, t, :], x1f[tp][:, :]), inc="sV",
                waits=[ev_lx, ev_acc_done, ev_outb_st[tp]])
            acc_prevL[t] = acc_prevR[t] = ev_o
            ev_x1f_rd[tp] = ev_o
            seq, r = row0 // S, row0 % S
            ev_out_final = P.emit("gpsimd", lambda e, seq=seq, r=r, tp=tp: e.dma_start(
                y_h[seq, r:r + 128, :], outb[tp][:, :]), inc="gO%d" % tp, waits=[ev_o])
            ev_outb_st[tp] = ev_out_final

    if debug:
        yx1 = nc.dram_tensor("y_x1", [T, D], F32, kind="ExternalOutput")
        ycb = nc.dram_tensor("y_comb", [128, NTT * NE], F32, kind="ExternalOutput")
        yxn = nc.dram_tensor("y_xn2T", [KT, 128, T], BF16, kind="ExternalOutput")
        ev_d1 = P.emit("gpsimd", lambda e: e.dma_start(yx1[:, :], scr_x1[:, :]),
                       inc="dB", waits=[ev_out_final] + list(ev_x1_stored))
        ev_d2 = P.emit("gpsimd", lambda e: e.dma_start(ycb[:, :], combine[:, :]),
                       inc="dB")
        ev_d3 = P.emit("gpsimd", lambda e: e.dma_start(
            yxn.rearrange("k p t -> p k t"), scr_xn2T.rearrange("k p t -> p k t")),
            inc="dB", waits=list(ev_xn2T_stored))
        P.wait("gpsimd", ev_d3)
    P.wait("gpsimd", ev_outb_st[0])
    P.wait("gpsimd", ev_outb_st[1])
    f_es.close()

    with es:
        sems = {s: es.enter_context(nc.semaphore(s))
                for s in P.cnt}
        with nc.Block() as block:
            @block.sync
            def _(sync):
                P.run("sync", sync, sems)

            @block.tensor
            def _(tensor):
                P.run("tensor", tensor, sems)

            @block.vector
            def _(vector):
                P.run("vector", vector, sems)

            @block.scalar
            def _(scalar):
                P.run("scalar", scalar, sems)

            @block.gpsimd
            def _(gpsimd):
                P.run("gpsimd", gpsimd, sems)

    return nc


_NC_CACHE = {}


def kernel(L0_attn_x, L0_attn_gamma, L0_attn_wq, L0_attn_wk, L0_attn_wv,
           L0_attn_wa, L0_attn_wg, L0_attn_wo, L0_ffn_gate_weight,
           L0_ffn_e_score_correction_bias, L0_ffn_experts_gate_up,
           L0_ffn_experts_down, L0_ffn_shared_gate_w, L0_ffn_shared_up_w,
           L0_ffn_shared_down_w):
    f = lambda a: np.ascontiguousarray(np.asarray(a, dtype=np.float32))
    x = f(L0_attn_x)
    B, S, _ = x.shape
    n_cores = 8
    nseq = B // n_cores
    key = (nseq, S)
    if key not in _NC_CACHE:
        _NC_CACHE[key] = build_nc(nseq, S)
    nc = _NC_CACHE[key]
    base = {
        "gamma": np.ascontiguousarray(f(L0_attn_gamma).reshape(KT, 128).T),
        "wq": f(L0_attn_wq), "wk": f(L0_attn_wk), "wv": f(L0_attn_wv),
        "wa": f(L0_attn_wa), "wg": f(L0_attn_wg), "wo": f(L0_attn_wo),
        "gate_w": f(L0_ffn_gate_weight),
        "gate_b": f(L0_ffn_e_score_correction_bias).reshape(1, NE),
        "egu": f(L0_ffn_experts_gate_up), "edn": f(L0_ffn_experts_down),
        "sgw": f(L0_ffn_shared_gate_w), "suw": f(L0_ffn_shared_up_w),
        "sdw": f(L0_ffn_shared_down_w),
    }
    in_maps = []
    for ci in range(n_cores):
        m = dict(base)
        m["x"] = np.ascontiguousarray(x[ci * nseq:(ci + 1) * nseq])
        in_maps.append(m)
    res = run_bass_kernel_spmd(nc, in_maps, core_ids=list(range(n_cores)))
    out = np.concatenate([np.asarray(r["y"]) for r in res.results], axis=0)
    return out.astype(np.float32)
